# version 02 (final)
# baseline (speedup 1.0000x reference)
import math
import numpy as np
import concourse.bass as bass
import concourse.mybir as mybir
from concourse.bass import IndirectOffsetOnAxis
from concourse.bass_utils import run_bass_kernel_spmd

F32 = mybir.dt.float32
BF16 = mybir.dt.bfloat16
F16 = mybir.dt.float16
I32 = mybir.dt.int32
U32 = mybir.dt.uint32
AF = mybir.ActivationFunctionType
ALU = mybir.AluOpType
AX = mybir.AxisListType

N_CORES = 8
B_CORE = 2
T_SEQ = 2048
T_CORE = B_CORE * T_SEQ
NT = T_CORE // 128
NT_B = T_SEQ // 128
NT_H = NT_B // 2
T_HALF = 1024
D = 1024
KD = D // 128
LN_EPS = 1e-5

SBUF_TOP = 229344
ARENA_BYTES = 79808
ARENA_BASE = SBUF_TOP - ARENA_BYTES

ENGS = ("sync", "vector", "scalar", "tensor", "gpsimd")
N_DMA_SEMS = 12


class Prog:

    DMA_ENGS = ("sync", "gpsimd")

    def __init__(self, nc):
        self.nc = nc
        self.semh = {}
        self.sem = {}
        self.dsem = {}
        self.duse = {}
        self.val = {}
        for e in ENGS:
            if e in self.DMA_ENGS:
                self.dsem[e] = []
                self.duse[e] = 0
                for i in range(N_DMA_SEMS):
                    nm = f"dq_{e}_{i}"
                    self.semh[nm] = nc.alloc_semaphore(name=nm)
                    self.dsem[e].append(nm)
                    self.val[nm] = 0
            nm = f"ch_{e}"
            self.semh[nm] = nc.alloc_semaphore(name=nm)
            self.sem[e] = nm
            self.val[nm] = 0
        self.sat = {e: {} for e in ENGS}
        self.rec = {}

    def eng(self, name):
        return getattr(self.nc, name)

    def _wait(self, eng, sem_nm, val):
        if val <= 0 or self.sat[eng].get(sem_nm, 0) >= val:
            return
        self.eng(eng).wait_ge(self.semh[sem_nm], val)
        self.sat[eng][sem_nm] = val

    def emit(self, eng, fn, reads=(), writes=(), inc=1, dma=False):
        dma = dma or inc == 16
        deps = {}
        for b in reads:
            r = self.rec.get(b)
            if r and r["w"]:
                s2, v2 = r["w"]
                deps[s2] = max(deps.get(s2, 0), v2)
        for b in writes:
            r = self.rec.get(b)
            if r:
                if r["w"]:
                    s2, v2 = r["w"]
                    deps[s2] = max(deps.get(s2, 0), v2)
                for s2, v2 in r["r"].items():
                    deps[s2] = max(deps.get(s2, 0), v2)
        own = self.sem[eng]
        for s2 in sorted(deps):
            if s2 == own and eng == "tensor":
                continue
            self._wait(eng, s2, deps[s2])
        if dma:
            i = self.duse[eng]
            self.duse[eng] += 1
            snm = self.dsem[eng][i % N_DMA_SEMS]
            self._wait(eng, snm, self.val[snm])
        else:
            snm = own
        inst = fn(self.eng(eng))
        self.val[snm] += inc
        inst.then_inc(self.semh[snm], inc)
        v = self.val[snm]
        for b in reads:
            r = self.rec.setdefault(b, {"w": None, "r": {}})
            r["r"][snm] = max(r["r"].get(snm, 0), v)
        for b in writes:
            self.rec[b] = {"w": (snm, v), "r": {}}
        return inst

    def barrier_reset(self):
        snap = dict(self.val)
        for e in ENGS:
            for s2, v2 in snap.items():
                self._wait(e, s2, v2)
        self.rec = {}

    def finish(self, eng="sync"):
        for s2, v2 in self.val.items():
            self._wait(eng, s2, v2)


class Arena:

    def __init__(self, nc):
        self.nc = nc
        self.off = 0
        self.n = 0

    def reset(self):
        self.off = 0

    def take(self, shape, dtype, tag=""):
        bpp = 1
        for s in shape[1:]:
            bpp *= s
        bpp *= mybir.dt.size(dtype)
        assert self.off + bpp <= ARENA_BYTES, f"arena overflow: {tag} {self.off}+{bpp}"
        self.n += 1
        h = self.nc.alloc_sbuf_tensor_at(
            f"ar{self.n}_{tag}", list(shape), dtype, offset=ARENA_BASE + self.off)
        self.off += (bpp + 31) // 32 * 32
        return h.ap()

    def alias(self, off, shape, dtype, tag=""):
        self.n += 1
        h = self.nc.alloc_sbuf_tensor_at(
            f"ar{self.n}_{tag}", list(shape), dtype, offset=ARENA_BASE + off)
        return h.ap()


def cdiv(a, b):
    return (a + b - 1) // b


_PSN = [0]


def std_psums(ctx, st):
    nc = ctx["nc"]
    _PSN[0] += 1
    n = _PSN[0]
    ps = {}
    ps["ps1"] = st.enter_context(nc.psum_tensor(f"ps1_{n}", [128, 1024], F32)).ap()
    ps["ps2"] = st.enter_context(nc.psum_tensor(f"ps2_{n}", [128, 1024], F32)).ap()
    ps["ps3"] = st.enter_context(nc.psum_tensor(f"ps3_{n}", [128, 512], F32)).ap()
    ps["ps5"] = st.enter_context(nc.psum_tensor(f"ps5_{n}", [128, 64], F32)).ap()
    ps["pt"] = st.enter_context(nc.psum_tensor(f"pt_{n}", [128, 2048], BF16)).ap()
    ctx["ps"] = ps
    return ps


def emit_zero(ctx, ap, key, n):
    ctx["p"].emit("gpsimd", lambda g: g.iota(
        ap, pattern=[[0, n]], base=0, channel_multiplier=0,
        allow_small_or_imprecise_dtypes=True), writes=[key])


def emit_const_col(ctx, ap, key, value_int):
    ctx["p"].emit("gpsimd", lambda g: g.iota(
        ap, pattern=[[0, 1]], base=value_int, channel_multiplier=0,
        allow_small_or_imprecise_dtypes=True), writes=[key])



def make_consts():
    import ml_dtypes
    c = {}
    c["c_ident"] = np.eye(128, dtype=ml_dtypes.bfloat16)
    c["c_ones_row"] = np.ones((1, 128), dtype=ml_dtypes.bfloat16)
    c["c_ones_row_f32"] = np.ones((1, 128), dtype=np.float32)
    c["c_ident_f32"] = np.eye(128, dtype=np.float32)
    c["c_ones_col"] = np.ones((128, 1), dtype=ml_dtypes.bfloat16)
    return c


def declare_consts(ctx):
    nc, p = ctx["nc"], ctx["p"]
    ident = nc.alloc_sbuf_tensor("sb_ident", [128, 128], BF16).ap()
    ones_row = nc.alloc_sbuf_tensor("sb_ones_row", [1, 128], BF16).ap()
    ones_col = nc.alloc_sbuf_tensor("sb_ones_col", [128, 1], BF16).ap()
    ones_row_f32 = nc.alloc_sbuf_tensor("sb_ones_row_f32", [1, 128], F32).ap()
    ident_f32 = nc.alloc_sbuf_tensor("sb_ident_f32", [128, 128], F32).ap()
    ctx["ident_f32"] = ident_f32
    ctx["ident_bf"] = ident
    ctx["ones_row_bf"] = ones_row
    ctx["ones_col_bf"] = ones_col
    ctx["ones_row_f32"] = ones_row_f32
    for nm, ap, dt_ in (("c_ident", ident, BF16), ("c_ones_row", ones_row, BF16),
                        ("c_ones_col", ones_col, BF16), ("c_ones_row_f32", ones_row_f32, F32),
                        ("c_ident_f32", ident_f32, F32)):
        hbm = nc.dram_tensor(nm, list(ap.shape), dt_, kind="ExternalInput").ap()
        p.emit("sync", lambda s, ap=ap, hbm=hbm: s.dma_start(out=ap, in_=hbm),
               writes=[nm], inc=16)



def emit_ln_tile(ctx, x_tile_ap, x_key, dst_ap, dst_key, dim=D):
    p = ctx["p"]
    stat = ctx["stat"]
    nchunk = dim // 512
    for c in range(nchunk):
        p.emit("vector", lambda v, c=c: v.bn_stats(
            stat[:, 6 * c:6 * c + 6], x_tile_ap[:, 512 * c:512 * (c + 1)]),
            reads=[x_key], writes=[("st6", c)])
    p.emit("vector", lambda v: v.bn_aggr(stat[:, 12:14], stat[:, 0:6 * nchunk]),
           reads=[("st6", c) for c in range(nchunk)], writes=["stmv"])
    p.emit("vector", lambda v: v.tensor_single_scalar(stat[:, 14:15], stat[:, 13:14], LN_EPS, ALU.add),
           reads=["stmv"], writes=["strs"])
    p.emit("scalar", lambda s: s.activation(stat[:, 14:15], stat[:, 14:15], AF.Ln),
           reads=["strs"], writes=["strs"])
    p.emit("scalar", lambda s: s.activation(stat[:, 14:15], stat[:, 14:15], AF.Exp, scale=-0.5),
           reads=["strs"], writes=["strs"])
    p.emit("vector", lambda v: v.tensor_scalar(stat[:, 15:16], stat[:, 12:13], stat[:, 14:15], -1.0, ALU.mult, ALU.mult),
           reads=["stmv", "strs"], writes=["stnm"])
    p.emit("vector", lambda v: v.tensor_scalar(dst_ap, x_tile_ap, stat[:, 14:15], stat[:, 15:16], ALU.mult, ALU.add),
           reads=[x_key, "strs", "stnm"], writes=[dst_key])


def emit_tile_tr(ctx, src_ap, src_key, dst_view, dst_keys, nblk, slot=0):
    p = ctx["p"]
    ident = ctx["ident_bf"]
    pt = ctx["ps"]["pt"]
    pv = pt[:, slot * 1024: slot * 1024 + nblk * 128]
    pk = ("ptT", slot)

    def trs(t):
        last = None
        for k in range(nblk):
            last = t.transpose(pv[:, k * 128:(k + 1) * 128], src_ap[:, k * 128:(k + 1) * 128], ident)
        return last
    p.emit("tensor", trs, reads=[src_key, "c_ident"], writes=[pk])
    srcv = pv.rearrange("p (k t) -> p k t", t=128)
    p.emit("vector", lambda v: v.tensor_copy(dst_view, srcv),
           reads=[pk], writes=dst_keys)


def emit_xfm(ctx, tiles, xfm, xfm_name, dst_T):
    sb = ctx["sb"]
    for j, ti in enumerate(tiles):
        slot = j % 2
        xh = sb["xh"][:, slot * D:(slot + 1) * D]
        emit_ln_tile(ctx, ctx["x_sb"][:, ti * D:(ti + 1) * D], ("x", ti), xh, ("xh", slot))
        dst = xfm.rearrange("p (k t) -> p k t", t=dst_T)[:, 0:KD, j * 128:(j + 1) * 128]
        emit_tile_tr(ctx, xh, ("xh", slot), dst, [(xfm_name, k) for k in range(KD)], KD, slot=slot)


def emit_load_w(ctx, hbm_ap, w_sb, w_name, FI, FO, fo0=0, fow=None, w_sb_fo=None):
    p = ctx["p"]
    if fow is None:
        fow = FO
    if w_sb_fo is None:
        w_sb_fo = fow
    for k in range(FI // 128):
        p.emit("gpsimd", lambda g, k=k: g.dma_start(
            out=w_sb[:, k * w_sb_fo:k * w_sb_fo + fow],
            in_=hbm_ap[k * 128:(k + 1) * 128, fo0:fo0 + fow]),
            writes=[(w_name, k)], inc=16)



def build_l0_attn_b(ctx):
    from contextlib import ExitStack
    with ExitStack() as _st:
        _build_l0_attn_b(ctx, std_psums(ctx, _st))
    ctx["p"].barrier_reset()


def _build_l0_attn_b(ctx, ps):
    p = ctx["p"]
    x_sb = ctx["x_sb"]
    sb = ctx["sb"]
    H = 8
    IN = ctx["in"]
    ar = ctx["ar"]
    ar.reset()

    xfm = ar.take([128, KD * T_HALF], BF16, "ab_xfm")
    wslab = ar.take([128, 16 * 1024], BF16, "ab_wslab")
    qfm = wslab[:, KD * D:]
    sc1 = ar.take([128, 1024], F32, "ab_sc1")
    sc2 = ar.take([128, 1024], F32, "ab_sc2")
    kvt = ar.take([128, 2048], BF16, "ab_kvt")
    kvacc = ar.take([128, 1024], F32, "ab_kvacc")
    kv_sb = ar.take([128, 1024], BF16, "ab_kv")
    small = ar.take([128, 32], BF16, "ab_small")
    smallf = ar.take([128, 32], F32, "ab_smallf")
    brow = ar.take([1, 3 * 1024], BF16, "ab_brow")
    sb["xh"] = ar.take([128, 2 * D], BF16, "ab_xh")
    ksum_sb = small[:, 0:KD]
    zcol = smallf[:, 0:H]
    ksacc = smallf[:, 16:16 + KD]
    bqc = ctx["cols"]

    ps1, ps2, ps5 = ps["ps1"], ps["ps2"], ps["ps5"]

    p.emit("gpsimd", lambda g: g.dma_start(out=brow[0:1, 0:D], in_=IN["L0_attn_b_bq"][None, :]), writes=["brq"], inc=16)
    p.emit("gpsimd", lambda g: g.dma_start(out=brow[0:1, D:2 * D], in_=IN["L0_attn_b_bk"][None, :]), writes=["brk"], inc=16)
    p.emit("gpsimd", lambda g: g.dma_start(out=brow[0:1, 2 * D:3 * D], in_=IN["L0_attn_b_bv"][None, :]), writes=["brv"], inc=16)
    for fo in range(KD):
        p.emit("sync", lambda s, fo=fo: s.dma_start(out=bqc[:, fo:fo + 1], in_=IN["L0_attn_b_bq"][fo * 128:(fo + 1) * 128][:, None]), writes=["bqc"], inc=16)
    p.emit("vector", lambda v: v.tensor_single_scalar(bqc[:, KD:2 * KD], bqc[:, 0:KD], -1.0, ALU.mult),
           reads=["bqc"], writes=["bqcn"])

    def fmap(src_ap, src_key, dst_ap, dst_key, n, bias_col=None, nbias_col=None):
        if nbias_col is None:
            p.emit("scalar", lambda s: s.activation(sc1[:, 0:n], src_ap, AF.Relu, scale=-1.0),
                   reads=[src_key], writes=["sc1"])
        else:
            p.emit("scalar", lambda s: s.activation(sc1[:, 0:n], src_ap, AF.Relu, bias=nbias_col, scale=-1.0),
                   reads=[src_key, "bqcn"], writes=["sc1"])
        p.emit("scalar", lambda s: s.activation(sc1[:, 0:n], sc1[:, 0:n], AF.Exp, scale=-1.0),
               reads=["sc1"], writes=["sc1"])
        if bias_col is None:
            p.emit("scalar", lambda s: s.activation(sc2[:, 0:n], src_ap, AF.Relu),
                   reads=[src_key], writes=["sc2"])
        else:
            p.emit("scalar", lambda s: s.activation(sc2[:, 0:n], src_ap, AF.Relu, bias=bias_col),
                   reads=[src_key, "bqc"], writes=["sc2"])
        p.emit("vector", lambda v: v.tensor_add(dst_ap, sc1[:, 0:n], sc2[:, 0:n]),
               reads=["sc1", "sc2"], writes=[dst_key])

    for b in range(B_CORE):
        emit_zero(ctx, kvacc[:, 0:512], "kvacc", 512)
        emit_zero(ctx, kvacc[:, 512:1024], "kvacc2", 512)
        emit_zero(ctx, ksacc[:, 0:KD], "ksacc", KD)
        emit_load_w(ctx, IN["L0_attn_b_Wk"], wslab, "wslab", D, D, w_sb_fo=D)
        emit_load_w(ctx, IN["L0_attn_b_Wv"], wslab[:, KD * D:], "wslabv", D, D, w_sb_fo=D)
        for half in range(2):
            tiles = [b * NT_B + half * NT_H + i for i in range(NT_H)]
            emit_xfm(ctx, tiles, xfm, "xfm", T_HALF)
            for si in range(NT_H):
                def mmk(t, si=si):
                    last = None
                    for nh in (0, 512):
                        for k in range(KD):
                            t.matmul(ps1[:, nh:nh + 512], xfm[:, k * T_HALF + si * 128: k * T_HALF + (si + 1) * 128],
                                     wslab[:, k * D + nh:k * D + nh + 512], start=(k == 0), stop=False)
                        last = t.matmul(ps1[:, nh:nh + 512], ctx["ones_row_bf"][0:1, 0:128], brow[0:1, D + nh:D + nh + 512], start=False, stop=True)
                    return last
                p.emit("tensor", mmk,
                       reads=[("xfm", k) for k in range(KD)] + [("wslab", k) for k in range(KD)] + ["brk", "c_ones_row"],
                       writes=["ps1"])
                fmap(ps1[:], "ps1", kvt[:, 0:D], "kvt_k", D)

                def mmv(t, si=si):
                    last = None
                    for nh in (0, 512):
                        for k in range(KD):
                            t.matmul(ps2[:, nh:nh + 512], xfm[:, k * T_HALF + si * 128: k * T_HALF + (si + 1) * 128],
                                     wslab[:, (KD + k) * D + nh:(KD + k) * D + nh + 512], start=(k == 0), stop=False)
                        last = t.matmul(ps2[:, nh:nh + 512], ctx["ones_row_bf"][0:1, 0:128], brow[0:1, 2 * D + nh:2 * D + nh + 512], start=False, stop=True)
                    return last
                p.emit("tensor", mmv,
                       reads=[("xfm", k) for k in range(KD)] + [("wslabv", k) for k in range(KD)] + ["brv", "c_ones_row"],
                       writes=["ps2"])
                p.emit("vector", lambda v: v.tensor_copy(kvt[:, D:2 * D], ps2[:]),
                       reads=["ps2"], writes=["kvt_v"])

                def mmkv(t):
                    last = None
                    for h in range(H):
                        last = t.matmul(ps2[:, h * 128:(h + 1) * 128], kvt[:, h * 128:(h + 1) * 128],
                                        kvt[:, D + h * 128: D + (h + 1) * 128], start=True, stop=True)
                    return last
                p.emit("tensor", mmkv, reads=["kvt_k", "kvt_v"], writes=["ps2"])

                def mmks(t):
                    last = None
                    for k in range(KD):
                        last = t.matmul(ps5[:, k:k + 1], kvt[:, k * 128:(k + 1) * 128],
                                        ctx["ones_col_bf"][0:128, 0:1], start=True, stop=True)
                    return last
                p.emit("tensor", mmks, reads=["kvt_k", "c_ones_col"], writes=["ps5"])
                p.emit("vector", lambda v: v.tensor_add(kvacc[:, 0:512], kvacc[:, 0:512], ps2[:, 0:512]),
                       reads=["ps2", "kvacc"], writes=["kvacc"])
                p.emit("vector", lambda v: v.tensor_add(kvacc[:, 512:1024], kvacc[:, 512:1024], ps2[:, 512:1024]),
                       reads=["ps2", "kvacc2"], writes=["kvacc2"])
                p.emit("vector", lambda v: v.tensor_add(ksacc[:, 0:KD], ksacc[:, 0:KD], ps5[:, 0:KD]),
                       reads=["ps5", "ksacc"], writes=["ksacc"])
        p.emit("vector", lambda v: v.tensor_copy(kv_sb[:, 0:512], kvacc[:, 0:512]), reads=["kvacc"], writes=["kv03"])
        p.emit("vector", lambda v: v.tensor_copy(kv_sb[:, 512:1024], kvacc[:, 512:1024]), reads=["kvacc2"], writes=["kv47"])
        p.emit("vector", lambda v: v.tensor_copy(ksum_sb[:], ksacc[:, 0:KD]), reads=["ksacc"], writes=["ksum"])

        p.barrier_reset()
        emit_load_w(ctx, IN["L0_attn_b_Wq"], wslab, "wslab", D, D, w_sb_fo=D)
        for half in range(2):
            tiles = [b * NT_B + half * NT_H + i for i in range(NT_H)]
            emit_xfm(ctx, tiles, xfm, "xfm", T_HALF)
            for fo in range(KD):
                def mmq(t, fo=fo):
                    last = None
                    for nh in (0, 512):
                        for k in range(KD):
                            last = t.matmul(ps1[:, nh:nh + 512], wslab[:, k * D + fo * 128: k * D + (fo + 1) * 128],
                                            xfm[:, k * T_HALF + nh:k * T_HALF + nh + 512],
                                            start=(k == 0), stop=(k == KD - 1))
                    return last
                p.emit("tensor", mmq,
                       reads=[("xfm", k) for k in range(KD)] + [("wslab", k) for k in range(KD)],
                       writes=["ps1"])
                fmap(ps1[:], "ps1", qfm[:, fo * T_HALF:(fo + 1) * T_HALF], ("qfm", fo), T_HALF,
                     bias_col=bqc[:, fo:fo + 1], nbias_col=bqc[:, KD + fo:KD + fo + 1])
            for si in range(NT_H):
                ti = b * NT_B + half * NT_H + si

                def mmqd(t, si=si):
                    last = None
                    for h in range(H):
                        last = t.matmul(ps5[:, h:h + 1],
                                        qfm[:, h * T_HALF + si * 128: h * T_HALF + (si + 1) * 128],
                                        ksum_sb[:, h:h + 1], start=True, stop=True)
                    return last
                p.emit("tensor", mmqd, reads=[("qfm", h) for h in range(H)] + ["ksum"], writes=["ps5"])
                p.emit("vector", lambda v: v.tensor_single_scalar(zcol[:, 0:H], ps5[:, 0:H], 1e-6, ALU.add),
                       reads=["ps5"], writes=["zcol"])
                p.emit("vector", lambda v: v.reciprocal(zcol[:, 0:H], zcol[:, 0:H]),
                       reads=["zcol"], writes=["zcol"])
                for h in range(H):
                    pso = ps1 if h % 2 == 0 else ps2
                    pk = "ps1" if h % 2 == 0 else "ps2"

                    def mmo(t, si=si, h=h, pso=pso):
                        return t.matmul(pso[:, 0:128],
                                        qfm[:, h * T_HALF + si * 128: h * T_HALF + (si + 1) * 128],
                                        kv_sb[:, h * 128:(h + 1) * 128], start=True, stop=True)
                    p.emit("tensor", mmo, reads=[("qfm", h), "kv03" if h < 4 else "kv47"], writes=[pk])
                    xs = x_sb[:, ti * D + h * 128: ti * D + (h + 1) * 128]
                    p.emit("vector", lambda v, xs=xs, h=h, pso=pso: v.scalar_tensor_tensor(
                        xs, pso[:, 0:128], zcol[:, h:h + 1], xs, ALU.mult, ALU.add),
                        reads=[pk, "zcol", ("x", ti)], writes=[("x", ti)])
        p.barrier_reset()



def build_l1_attn_a(ctx):
    from contextlib import ExitStack
    with ExitStack() as _st:
        _build_l1_attn_a(ctx, std_psums(ctx, _st))
    ctx["p"].barrier_reset()


def _build_l1_attn_a(ctx, ps):
    p = ctx["p"]
    x_sb = ctx["x_sb"]
    sb = ctx["sb"]
    IN = ctx["in"]
    ar = ctx["ar"]
    ar.reset()
    H, DH = 16, 64
    NQ = 4

    xfm = ar.take([128, KD * T_SEQ], BF16, "aa_xfm")
    strip = ar.take([128, 3968], F16, "aa_strip")
    vsb = ar.take([128, NT_B * 65], BF16, "aa_v")
    borow = ar.take([1, 1024], BF16, "aa_borow")
    mark = ar.off
    qkbuf = ar.take([128, 2 * T_SEQ], BF16, "aa_qk")
    etile = ar.take([128, 2 * 512], BF16, "aa_et")
    wh = ar.take([128, 2 * KD * 192], BF16, "aa_wh")
    sc = ar.take([128, 2 * 512], F32, "aa_sc")
    scb = ar.take([128, 512], F32, "aa_scb")
    att = ar.take([128, T_SEQ], BF16, "aa_att")
    rrow = ar.take([1, 512], F32, "aa_rrow")
    sb["xh"] = ar.take([128, 2 * D], BF16, "aa_xh")
    ar.off = mark
    aob = ar.take([128, 4 * 2048], BF16, "aa_aob")
    cols = ctx["cols"]

    ps1, ps2, ps3, ps5 = ps["ps1"], ps["ps2"], ps["ps3"], ps["ps5"]
    scr = ctx["scr_attnout"]

    slopes = [2.0 ** (-8.0 * h / H) for h in range(H)]

    p.emit("gpsimd", lambda g: g.iota(strip[:], pattern=[[1, 3968]], base=-1920,
                                      channel_multiplier=-1, allow_small_or_imprecise_dtypes=True),
           writes=["strip"])
    p.emit("scalar", lambda s: s.activation(strip[:], strip[:], AF.Abs),
           reads=["strip"], writes=["strip"])
    p.emit("gpsimd", lambda g: g.iota(
        vsb.rearrange("p (k c) -> p k c", c=65)[:, :, 64:65], pattern=[[0, NT_B], [0, 1]],
        base=1, channel_multiplier=0, allow_small_or_imprecise_dtypes=True),
        writes=["vones"])
    mC = cols[:, 32:33]
    emit_const_col(ctx, mC, "mC", -8)
    for j in range(24):
        p.emit("sync", lambda s, j=j: s.dma_start(out=cols[:, j:j + 1], in_=IN["L1_attn_a_b_qkv"][j * 128:(j + 1) * 128][:, None]),
               writes=["bcols"], inc=16)
    p.emit("vector", lambda v: v.tensor_single_scalar(cols[:, 0:8], cols[:, 0:8], 0.125, ALU.mult),
           reads=["bcols"], writes=["bcols"])
    p.emit("gpsimd", lambda g: g.dma_start(out=borow[0:1, 0:D], in_=IN["L1_attn_a_b_out"][None, :]),
           writes=["borow"], inc=16)

    for b in range(B_CORE):
        emit_xfm(ctx, [b * NT_B + i for i in range(NT_B)], xfm, "xfm", T_SEQ)
        for h in range(H):
            par = h % 2
            hb = 64 * par
            wv_ = wh[:, par * KD * 192:(par + 1) * KD * 192]
            emit_load_w(ctx, IN["L1_attn_a_W_qkv_hpack"], wv_, ("whk", par), D, 16 * 192,
                        fo0=h * 192, fow=192, w_sb_fo=192)
            for which in range(2):
                woff = which * 64
                gi = which * 1024 + h * 64
                ci, r0 = gi // 128, gi % 128
                for c0 in range(0, T_SEQ, 512):
                    def mmp(t, woff=woff, c0=c0, wv_=wv_, hb=hb):
                        last = None
                        for k in range(KD):
                            last = t.matmul(ps2[hb:hb + 64, 0:512],
                                            wv_[:, k * 192 + woff: k * 192 + woff + 64],
                                            xfm[:, k * T_SEQ + c0:k * T_SEQ + c0 + 512],
                                            start=(k == 0), stop=(k == KD - 1))
                        return last
                    p.emit("tensor", mmp,
                           reads=[(("whk", par), k) for k in range(KD)] + [("xfm", k) for k in range(KD)], writes=["ps2p"])
                    dst = qkbuf[hb:hb + 64, which * T_SEQ + c0: which * T_SEQ + c0 + 512]
                    p.emit("vector", lambda v, dst=dst, ci=ci, r0=r0, which=which, hb=hb: v.tensor_scalar(
                        dst, ps2[hb:hb + 64, 0:512],
                        0.125 if which == 0 else 1.0, cols[r0:r0 + 64, ci:ci + 1], ALU.mult, ALU.add),
                        reads=["ps2p", "bcols"], writes=[("qk", par, which)])
            gi = 2 * 1024 + h * 64
            ci, r0 = gi // 128, gi % 128
            for c0 in range(0, T_SEQ, 512):
                def mmpv(t, c0=c0, wv_=wv_):
                    last = None
                    for k in range(KD):
                        last = t.matmul(ps2[0:64, 0:512],
                                        wv_[:, k * 192 + 128: k * 192 + 192],
                                        xfm[:, k * T_SEQ + c0:k * T_SEQ + c0 + 512],
                                        start=(k == 0), stop=(k == KD - 1))
                    return last
                p.emit("tensor", mmpv, reads=[(("whk", par), k) for k in range(KD)] + [("xfm", k) for k in range(KD)], writes=["ps2p"])
                p.emit("vector", lambda v, c0=c0, ci=ci, r0=r0: v.tensor_scalar(
                    sb["xh"][0:64, c0:c0 + 512], ps2[0:64, 0:512], 1.0, cols[r0:r0 + 64, ci:ci + 1], ALU.mult, ALU.add),
                    reads=["ps2p", "bcols"], writes=["vfm"])

            def trv(t):
                last = None
                for kt in range(NT_B):
                    last = t.transpose(ps["pt"][:, kt * 64:(kt + 1) * 64],
                                       sb["xh"][0:64, kt * 128:(kt + 1) * 128], ctx["ident_bf"][0:64, 0:64])
                return last
            p.emit("tensor", trv, reads=["vfm", "c_ident"], writes=[("ptT", 0)])
            p.emit("vector", lambda v: v.tensor_copy(
                vsb.rearrange("p (k c) -> p k c", c=65)[:, :, 0:64],
                ps["pt"][:, 0:NT_B * 64].rearrange("p (k c) -> p k c", c=64)),
                reads=[("ptT", 0)], writes=["vtok"])

            for qc in range(NQ):
                q0 = qc * 512
                for kt in range(NT_B):
                    sps = ps1[:, (kt % 2) * 512:(kt % 2) * 512 + 512]
                    spk = ("stp", kt % 2)

                    def mmst(t, kt=kt, q0=q0, sps=sps, hb=hb):
                        return t.matmul(sps, qkbuf[hb:hb + 64, T_SEQ + kt * 128: T_SEQ + (kt + 1) * 128],
                                        qkbuf[hb:hb + 64, q0:q0 + 512], start=True, stop=True)
                    p.emit("tensor", mmst, reads=[("qk", par, 0), ("qk", par, 1)], writes=[spk])
                    stf = strip[:, q0 - kt * 128 + 1920: q0 - kt * 128 + 1920 + 512]
                    scv = sc[:, (kt % 2) * 512:(kt % 2) * 512 + 512]
                    sck = ("scp", kt % 2)
                    p.emit("vector", lambda v, stf=stf, scv=scv, sps=sps, h=h: v.scalar_tensor_tensor(
                        scv, stf, -slopes[h], sps, ALU.mult, ALU.add),
                        reads=[spk, "strip"], writes=[sck])
                    ev = etile[:, (kt % 2) * 512:(kt % 2) * 512 + 512]
                    evk = ("et", kt % 2)
                    p.emit("scalar", lambda s, ev=ev, scv=scv: s.activation(ev, scv, AF.Exp, bias=mC),
                           reads=[sck, "mC"], writes=[evk])

                    def mmpv2(t, kt=kt, ev=ev):
                        return t.matmul(ps3[0:65, 0:512], vsb[:, kt * 65:(kt + 1) * 65], ev,
                                        start=(kt == 0), stop=(kt == NT_B - 1))
                    p.emit("tensor", mmpv2, reads=[evk, "vtok", "vones"], writes=["ps_o"])
                p.emit("vector", lambda v: v.reciprocal(rrow[0:1, 0:512], ps3[64:65, 0:512]),
                       reads=["ps_o"], writes=["rrow"])

                def mmbc(t):
                    return t.matmul(ps2[0:64, 0:512], ctx["ones_row_f32"][0:1, 0:64], rrow[0:1, 0:512],
                                    start=True, stop=True)
                p.emit("tensor", mmbc, reads=["rrow", "c_ones_row"], writes=["ps2p"])
                p.emit("vector", lambda v: v.tensor_copy(scb[0:64, 0:512], ps2[0:64, 0:512]),
                       reads=["ps2p"], writes=["scb"])
                p.emit("vector", lambda v, q0=q0: v.scalar_tensor_tensor(
                    att[0:64, q0:q0 + 512], ps3[0:64, 0:512], 1.0, scb[0:64, 0:512], ALU.bypass, ALU.mult),
                    reads=["ps_o", "scb"], writes=["att"])
            p.emit("sync", lambda s, b=b, h=h: s.dma_start(
                out=scr[b * 1024 + h * 64: b * 1024 + (h + 1) * 64, :], in_=att[0:64, :]),
                reads=["att"], writes=["scr"], inc=16)

        p.barrier_reset()
        emit_load_w(ctx, IN["L1_attn_a_W_out_pack"], xfm, "wout", D, D, w_sb_fo=D)
        for kp in range(2):
            for k4 in range(4):
                k = kp * 4 + k4
                p.emit("sync", lambda s, k=k, k4=k4, b=b: s.dma_start(
                    out=aob[:, k4 * 2048:(k4 + 1) * 2048],
                    in_=scr[b * 1024 + k * 128: b * 1024 + (k + 1) * 128, :]),
                    reads=["scr"], writes=[("ao", k4)], inc=16)
            for m in range(NT_B):
                ti = b * NT_B + m

                def mmop(t, m=m, kp=kp):
                    last = None
                    for nh in (0, 512):
                        for k4 in range(4):
                            k = kp * 4 + k4
                            last = t.matmul(ps1[:, nh:nh + 512],
                                     aob[:, k4 * 2048 + m * 128: k4 * 2048 + (m + 1) * 128],
                                     xfm[:, k * D + nh: k * D + nh + 512], start=(k4 == 0),
                                     stop=(kp == 1 and k4 == 3))
                        if kp == 0:
                            last = t.matmul(ps1[:, nh:nh + 512], ctx["ones_row_bf"][0:1, 0:128],
                                            borow[0:1, nh:nh + 512], start=False, stop=True)
                    return last
                p.emit("tensor", mmop,
                       reads=[("ao", k4) for k4 in range(4)] + [("wout", k) for k in range(KD)] + ["borow", "c_ones_row"],
                       writes=["ps1"])
                xs = x_sb[:, ti * D:(ti + 1) * D]
                p.emit("vector", lambda v, xs=xs: v.tensor_add(xs, xs, ps1[:]),
                       reads=["ps1", ("x", ti)], writes=[("x", ti)])
            p.barrier_reset()





FFN_L0 = dict(pre="L0_ffn", E=8, NIH=11, NSH=22, eps=1e-20, sh_gate=False)
FFN_L1 = dict(pre="L1_ffn", E=16, NIH=4, NSH=22, eps=0.0, sh_gate=True)


def build_ffn(ctx, cfg):
    from contextlib import ExitStack
    with ExitStack() as _st:
        _build_ffn(ctx, cfg, std_psums(ctx, _st))
    ctx["p"].barrier_reset()


def _build_ffn(ctx, cfg, ps):
    p = ctx["p"]
    x_sb = ctx["x_sb"]
    IN = ctx["in"]
    ar = ctx["ar"]
    ar.reset()
    pre, E, NIH, NSH = cfg["pre"], cfg["E"], cfg["NIH"], cfg["NSH"]
    EG = E + (1 if cfg["sh_gate"] else 0)
    NG = 4
    TG = 1024
    MG = TG // 128

    xfm = ar.take([128, KD * TG], BF16, pre + "_xfm")
    hbuf = ar.take([128, 11264], BF16, pre + "_h")
    wd_sb = ar.take([128, 11264], BF16, pre + "_wd")
    wgu = ar.take([128, 2048], BF16, pre + "_wgu")
    f32a = ar.take([128, 1024], F32, pre + "_f32a")
    f32b = ar.take([128, 1024], F32, pre + "_f32b")
    gsb = ar.take([128, 8 * EG], F32, pre + "_g")
    wdense = ar.take([128, NT * (E + 1)], F32, pre + "_wdense")
    ssc = ar.take([128, 8 * EG], F32, pre + "_ssc")
    stat = ctx["stat"]

    ps1, ps2, ps3, ps5 = ps["ps1"], ps["ps2"], ps["ps3"], ps["ps5"]
    scr_y = ctx["scr_y"]
    EW = E + 1

    p.emit("sync", lambda s: s.dma_start(out=gsb[:], in_=IN[pre + "_gate_pack"]),
           writes=["gsb"], inc=16)

    for g in range(NG):
        tiles = [g * MG + m for m in range(MG)]
        for j, ti in enumerate(tiles):
            emit_ln_tile(ctx, x_sb[:, ti * D:(ti + 1) * D], ("x", ti), f32a[:], "f32a")

            def tr32(t):
                last = None
                for k in range(KD):
                    last = t.transpose(ps1[:, k * 128:(k + 1) * 128], f32a[:, k * 128:(k + 1) * 128],
                                       ctx["ident_f32"])
                return last
            p.emit("tensor", tr32, reads=["f32a", "c_ident_f32"], writes=["ps1"])
            p.emit("vector", lambda v: v.tensor_copy(f32b[:], ps1[:]), reads=["ps1"], writes=["f32b"])
            dst = xfm.rearrange("p (k t) -> p k t", t=TG)[:, 0:KD, j * 128:(j + 1) * 128]
            p.emit("vector", lambda v, dst=dst: v.tensor_copy(dst, ps1[:].rearrange("p (k t) -> p k t", t=128)),
                   reads=["ps1"], writes=[("xfm", k2) for k2 in range(KD)])

            def mmg(t):
                last = None
                for k in range(KD):
                    last = t.matmul(ps5[:, 0:EG], f32b[:, k * 128:(k + 1) * 128],
                                    gsb[:, k * EG:(k + 1) * EG], start=(k == 0), stop=(k == KD - 1))
                return last
            p.emit("tensor", mmg, reads=["f32b", "gsb"], writes=["ps5"])
            G = 3 * E
            p.emit("vector", lambda v: v.tensor_reduce(ssc[:, G:G + 1], ps5[:, 0:E], AX.X, ALU.max),
                   reads=["ps5"], writes=["g_m"])
            p.emit("vector", lambda v: v.tensor_single_scalar(ssc[:, G + 1:G + 2], ssc[:, G:G + 1], -1.0, ALU.mult),
                   reads=["g_m"], writes=["g_nm"])
            p.emit("scalar", lambda s: s.activation(ssc[:, 0:E], ps5[:, 0:E], AF.Exp, bias=ssc[:, G + 1:G + 2],
                                                    accum_out=ssc[:, G + 2:G + 3]),
                   reads=["ps5", "g_nm"], writes=["g_e", "g_rs"])
            p.emit("vector", lambda v: v.reciprocal(ssc[:, G + 3:G + 4], ssc[:, G + 2:G + 3]),
                   reads=["g_rs"], writes=["g_rr"])
            p.emit("vector", lambda v: v.tensor_single_scalar(ssc[:, 0:E], ssc[:, 0:E], ssc[:, G + 3:G + 4], ALU.mult),
                   reads=["g_e", "g_rr"], writes=["g_s"])
            p.emit("vector", lambda v: v.tensor_reduce(ssc[:, G + 4:G + 5], ssc[:, 0:E], AX.X, ALU.max),
                   reads=["g_s"], writes=["g_m1"])
            p.emit("vector", lambda v: v.tensor_single_scalar(ssc[:, E:2 * E], ssc[:, 0:E], ssc[:, G + 4:G + 5], ALU.is_ge),
                   reads=["g_s", "g_m1"], writes=["g_ge1"])
            p.emit("vector", lambda v: v.scalar_tensor_tensor(ssc[:, 2 * E:3 * E], ssc[:, E:2 * E], -1e30,
                                                              ssc[:, 0:E], ALU.mult, ALU.add),
                   reads=["g_ge1", "g_s"], writes=["g_msk"])
            p.emit("vector", lambda v: v.tensor_reduce(ssc[:, G + 5:G + 6], ssc[:, 2 * E:3 * E], AX.X, ALU.max),
                   reads=["g_msk"], writes=["g_m2"])
            p.emit("vector", lambda v: v.tensor_single_scalar(ssc[:, E:2 * E], ssc[:, 0:E], ssc[:, G + 5:G + 6], ALU.is_ge),
                   reads=["g_s", "g_m2"], writes=["g_top2"])
            p.emit("vector", lambda v: v.tensor_add(ssc[:, G + 6:G + 7], ssc[:, G + 4:G + 5], ssc[:, G + 5:G + 6]),
                   reads=["g_m1", "g_m2"], writes=["g_dn"])
            if cfg["eps"]:
                p.emit("vector", lambda v: v.tensor_single_scalar(ssc[:, G + 6:G + 7], ssc[:, G + 6:G + 7], cfg["eps"], ALU.add),
                       reads=["g_dn"], writes=["g_dn"])
            p.emit("vector", lambda v: v.reciprocal(ssc[:, G + 6:G + 7], ssc[:, G + 6:G + 7]),
                   reads=["g_dn"], writes=["g_dn"])
            p.emit("vector", lambda v: v.tensor_mul(ssc[:, 0:E], ssc[:, 0:E], ssc[:, E:2 * E]),
                   reads=["g_s", "g_top2"], writes=["g_s"])
            p.emit("vector", lambda v, ti=ti: v.tensor_single_scalar(
                wdense[:, ti * EW:ti * EW + E], ssc[:, 0:E], ssc[:, G + 6:G + 7], ALU.mult),
                reads=["g_s", "g_dn"], writes=[("wd_c", ti)])
            if cfg["sh_gate"]:
                p.emit("scalar", lambda s, ti=ti: s.activation(wdense[:, ti * EW + E:ti * EW + E + 1],
                                                               ps5[:, E:E + 1], AF.Sigmoid),
                       reads=["ps5"], writes=[("wd_c", ti)])
            else:
                emit_const_col(ctx, wdense[:, ti * EW + E:ti * EW + E + 1], ("wd_c", ti), 1)

        def up_block(wg_hbm, wu_hbm, ih, nb, toks, t0=0):
            p.emit("gpsimd", lambda g2: g2.dma_start(
                out=wgu[:, 0:1024], in_=wg_hbm[:, ih * 1024:(ih + 1) * 1024]),
                writes=["wgu_g"], inc=16)
            p.emit("gpsimd", lambda g2: g2.dma_start(
                out=wgu[:, 1024:2048], in_=wu_hbm[:, ih * 1024:(ih + 1) * 1024]),
                writes=["wgu_u"], inc=16)
            for c0 in range(0, toks, 512):
                def mmhg(t, ih=ih, c0=c0):
                    last = None
                    for k in range(KD):
                        last = t.matmul(ps1[:, 0:512], wgu[:, k * 128:(k + 1) * 128],
                                        xfm[:, k * TG + t0 + c0: k * TG + t0 + c0 + 512],
                                        start=(k == 0), stop=(k == KD - 1))
                    return last
                p.emit("tensor", mmhg, reads=["wgu_g"] + [("xfm", k) for k in range(KD)], writes=["ps1h"])
                p.emit("scalar", lambda s, c0=c0: s.activation(f32a[:, 0:512], ps1[:, 0:512], AF.Silu),
                       reads=["ps1h"], writes=["f32a"])

                def mmhu(t, ih=ih, c0=c0):
                    last = None
                    for k in range(KD):
                        last = t.matmul(ps2[:, 0:512], wgu[:, 1024 + k * 128:1024 + (k + 1) * 128],
                                        xfm[:, k * TG + t0 + c0: k * TG + t0 + c0 + 512],
                                        start=(k == 0), stop=(k == KD - 1))
                    return last
                p.emit("tensor", mmhu, reads=["wgu_u"] + [("xfm", k) for k in range(KD)], writes=["ps2h"])
                p.emit("vector", lambda v, ih=ih, c0=c0, toks=toks: v.tensor_mul(
                    hbuf[:, ih * toks + c0: ih * toks + c0 + 512], f32a[:, 0:512], ps2[:, 0:512]),
                    reads=["f32a", "ps2h"], writes=["hbuf"])

        for e in range(E):
            wgh = IN[pre + "_wg_pack"][e]
            wuh = IN[pre + "_wu_pack"][e]
            p.emit("gpsimd", lambda g2, e=e: g2.dma_start(out=wd_sb[:, 0:NIH * 1024],
                                                          in_=IN[pre + "_wd_pack"][e]),
                   writes=["wd_sb"], inc=16)
            for ih in range(NIH):
                up_block(wgh, wuh, ih, NIH, TG)
            for m in range(MG):
                ti = g * MG + m
                for fh in range(2):
                    def mmdn(t, m=m, fh=fh):
                        last = None
                        for ih in range(NIH):
                            last = t.matmul(ps3[:, 0:512], hbuf[:, ih * TG + m * 128:(ih * TG + (m + 1) * 128)],
                                            wd_sb[:, ih * 1024 + fh * 512: ih * 1024 + fh * 512 + 512],
                                            start=(ih == 0), stop=(ih == NIH - 1))
                        return last
                    p.emit("tensor", mmdn, reads=["hbuf", "wd_sb"], writes=["ps3d"])
                    eo = f32b[:, fh * 512:(fh + 1) * 512]
                    p.emit("vector", lambda v, eo=eo, ti=ti, e=e: v.tensor_single_scalar(
                        eo, ps3[:, 0:512], wdense[:, ti * EW + e:ti * EW + e + 1], ALU.mult),
                        reads=["ps3d", ("wd_c", ti)], writes=["f32b"])
                    yap = scr_y[ti * 128:(ti + 1) * 128, fh * 512:(fh + 1) * 512]
                    if e == 0:
                        p.emit("gpsimd", lambda g2, yap=yap, eo=eo: g2.dma_start(out=yap, in_=eo),
                               reads=["f32b"], writes=[("y", ti, fh)], inc=16)
                    else:
                        p.emit("gpsimd", lambda g2, yap=yap, eo=eo: g2.dma_start(out=yap, in_=eo, accum_op=ALU.add),
                               reads=["f32b", ("y", ti, fh)], writes=[("y", ti, fh)], inc=16)

        for sh in range(2):
            t0 = sh * 512
            for ih in range(NSH):
                up_block(IN[pre + "_swg_pack"], IN[pre + "_swu_pack"], ih, NSH, 512, t0=t0)
            for fh in range(2):
                p.emit("gpsimd", lambda g2, fh=fh: g2.dma_start(out=wd_sb[:, 0:NSH * 512],
                                                                in_=IN[pre + "_swd_pack"][fh]),
                       writes=["wd_sb"], inc=16)
                for m2 in range(4):
                    m = sh * 4 + m2
                    ti = g * MG + m

                    def mmds(t, m2=m2, t0=t0):
                        last = None
                        for ih in range(NSH):
                            last = t.matmul(ps3[:, 0:512], hbuf[:, ih * 512 + m2 * 128:(ih * 512 + (m2 + 1) * 128)],
                                            wd_sb[:, ih * 512:(ih + 1) * 512],
                                            start=(ih == 0), stop=(ih == NSH - 1))
                        return last
                    p.emit("tensor", mmds, reads=["hbuf", "wd_sb"], writes=["ps3d"])
                    eo = f32b[:, 0:512]
                    p.emit("vector", lambda v, eo=eo, ti=ti: v.tensor_single_scalar(
                        eo, ps3[:, 0:512], wdense[:, ti * EW + E:ti * EW + E + 1], ALU.mult),
                        reads=["ps3d", ("wd_c", ti)], writes=["f32b"])
                    yap = scr_y[ti * 128:(ti + 1) * 128, fh * 512:(fh + 1) * 512]
                    p.emit("gpsimd", lambda g2, yap=yap, eo=eo: g2.dma_start(out=yap, in_=eo, accum_op=ALU.add),
                           reads=["f32b", ("y", ti, fh)], writes=[("y", ti, fh)], inc=16)

    for ti in range(NT):
        p.emit("sync", lambda s, ti=ti: s.dma_start(out=f32a[:], in_=scr_y[ti * 128:(ti + 1) * 128, :]),
               reads=[("y", ti, 0), ("y", ti, 1)], writes=["f32a"], inc=16)
        xs = x_sb[:, ti * D:(ti + 1) * D]
        p.emit("vector", lambda v, xs=xs: v.tensor_add(xs, xs, f32a[:]),
               reads=["f32a", ("x", ti)], writes=[("x", ti)])
    p.barrier_reset()


def build_l0_ffn(ctx):
    build_ffn(ctx, FFN_L0)


def build_l1_ffn(ctx):
    build_ffn(ctx, FFN_L1)






def build_l1_attn_b(ctx):
    from contextlib import ExitStack
    p = ctx["p"]
    x_sb = ctx["x_sb"]
    IN = ctx["in"]
    nc = ctx["nc"]
    ar = ctx["ar"]
    ar.reset()
    H = 16
    NTL = 32
    TC = 1024
    stat = ctx["stat"]

    xfm = ar.take([128, KD * T_SEQ], BF16, "sc_xfm")
    sel = ar.take([64, NTL * 128], BF16, "sc_sel")
    red = ar.take([128, NTL * 64], BF16, "sc_red")
    mark = ar.off
    whp = ar.take([128, KD * 192], BF16, "sc_whp")
    qkfm = ar.take([128, 2048], BF16, "sc_qkfm")
    qdup = ar.take([128, 2048], BF16, "sc_qdup")
    ktile = ar.take([128, 2048], BF16, "sc_ktile")
    bkv = ar.take([128, 2048], BF16, "sc_bkv")
    Abuf = ar.take([128, TC], F32, "sc_A")
    Bbuf = ar.take([128, TC], BF16, "sc_B")
    Sbuf = ar.take([128, TC], BF16, "sc_S")
    Mbuf = ar.take([128, TC], BF16, "sc_M")
    stage = Abuf[:, 0:512]
    brow = ar.take([1, 2048], BF16, "sc_brow")
    qstage = ar.take([128, 192], BF16, "sc_qs")
    sc = ar.take([128, 64], F32, "sc_sc")
    wbeta = ar.take([128, KD * 16], BF16, "sc_wb")
    carry = ctx["cols"][:, 0:32]
    scr_ob = ctx["scr_ob"]
    scr_res = ctx["scr_attnout"]

    p.emit("sync", lambda s: s.dma_start(out=sel[:], in_=IN["c_selred"][0:64, 0:NTL * 128]),
           writes=["c_sel"], inc=16)
    p.emit("sync", lambda s: s.dma_start(out=red[:], in_=IN["c_selred"][:, NTL * 128:NTL * 128 + NTL * 64]),
           writes=["c_red"], inc=16)
    emit_load_w(ctx, IN["L1_attn_b_Wbeta_pack"], wbeta, "wbetaf", D, 16, w_sb_fo=16)

    for b in range(B_CORE):
        ctx["sb"]["xh"] = qdup
        with ExitStack() as st0:
            std_psums(ctx, st0)
            emit_xfm(ctx, [b * NT_B + i for i in range(NT_B)], xfm, "xfm", T_SEQ)
        p.barrier_reset()
        for h in range(H):
            with ExitStack() as st:
                _PSN[0] += 1
                n_ = _PSN[0]
                psA = st.enter_context(nc.psum_tensor(f"psA_{n_}", [128, 512], F32)).ap()
                pt = st.enter_context(nc.psum_tensor(f"pt_{n_}", [128, 512], BF16)).ap()
                bc = st.enter_context(nc.psum_tensor(f"bc_{n_}", [128, TC], F32)).ap()
                reds = [st.enter_context(nc.psum_tensor(f"red{q_}_{n_}", [64, 512], F32)).ap() for q_ in range(4)]
                emit_load_w(ctx, IN["L1_attn_b_W_hpack"], whp, "whb", D, 16 * 192,
                            fo0=h * 192, fow=192, w_sb_fo=192)
                for c4 in range(4):
                    def mmb(t, c4=c4, h=h):
                        last = None
                        for k in range(KD):
                            last = t.matmul(psA[0:1, 0:512],
                                            wbeta[:, k * 16 + h:k * 16 + h + 1],
                                            xfm[:, k * T_SEQ + c4 * 512:k * T_SEQ + (c4 + 1) * 512],
                                            start=(k == 0), stop=(k == KD - 1))
                        return last
                    p.emit("tensor", mmb, reads=[("wbetaf", k) for k in range(KD)] + [("xfm", k) for k in range(KD)], writes=["psA"])
                    p.emit("scalar", lambda s, c4=c4: s.activation(
                        brow[0:1, c4 * 512:(c4 + 1) * 512], psA[0:1, 0:512], AF.Sigmoid),
                        reads=["psA"], writes=["brow"])
                for si in range(NT_B):
                    def mmp(t, si=si):
                        last = None
                        for k in range(KD):
                            last = t.matmul(psA[:, 0:192], xfm[:, k * T_SEQ + si * 128:k * T_SEQ + (si + 1) * 128],
                                            whp[:, k * 192:(k + 1) * 192], start=(k == 0), stop=(k == KD - 1))
                        return last
                    p.emit("tensor", mmp, reads=[("whb", k) for k in range(KD)] + [("xfm", k) for k in range(KD)], writes=["psA"])
                    for off in (0, 64):
                        p.emit("scalar", lambda s, off=off: s.activation(sc[:, 0:64], psA[:, off:off + 64], AF.Square),
                               reads=["psA"], writes=["nsq"])
                        p.emit("vector", lambda v: v.tensor_reduce(stat[:, 0:1], sc[:, 0:64], AX.X, ALU.add),
                               reads=["nsq"], writes=["nss"])
                        p.emit("scalar", lambda s: s.activation(stat[:, 1:2], stat[:, 0:1], AF.Sqrt),
                               reads=["nss"], writes=["nsr"])
                        p.emit("vector", lambda v: v.tensor_single_scalar(stat[:, 1:2], stat[:, 1:2], 1e-12, ALU.max),
                               reads=["nsr"], writes=["nsr"])
                        p.emit("vector", lambda v: v.reciprocal(stat[:, 1:2], stat[:, 1:2]),
                               reads=["nsr"], writes=["nsr"])
                        p.emit("vector", lambda v, off=off: v.tensor_single_scalar(
                            qstage[:, off:off + 64], psA[:, off:off + 64], stat[:, 1:2], ALU.mult),
                            reads=["psA", "nsr"], writes=["qstage"])
                    p.emit("vector", lambda v: v.tensor_copy(qstage[:, 128:192], psA[:, 128:192]),
                           reads=["psA"], writes=["qstage"])
                    def trqkv(t):
                        last = None
                        for j3 in range(3):
                            last = t.transpose(pt[0:64, j3 * 128:(j3 + 1) * 128],
                                               qstage[:, j3 * 64:(j3 + 1) * 64], ctx["ident_bf"])
                        return last
                    p.emit("tensor", trqkv, reads=["qstage", "c_ident"], writes=[("ptT", 0)])
                    p.emit("vector", lambda v, si=si: v.tensor_copy(qkfm[0:64, si * 128:(si + 1) * 128], pt[0:64, 0:128]),
                           reads=[("ptT", 0)], writes=["qkfm_q"])
                    p.emit("vector", lambda v, si=si: v.tensor_copy(qkfm[64:128, si * 128:(si + 1) * 128], pt[0:64, 128:256]),
                           reads=[("ptT", 0)], writes=["qkfm_k"])
                    p.emit("vector", lambda v, si=si: v.tensor_copy(bkv[64:128, si * 128:(si + 1) * 128], pt[0:64, 256:384]),
                           reads=[("ptT", 0)], writes=["bkv_v"])
                p.emit("vector", lambda v: v.tensor_copy(qdup[0:64, :], qkfm[0:64, :]), reads=["qkfm_q"], writes=["qdup"])
                p.emit("vector", lambda v: v.tensor_copy(qdup[64:128, :], qkfm[0:64, :]), reads=["qkfm_q"], writes=["qdup"])
                p.emit("vector", lambda v: v.tensor_copy(ktile[0:64, :], qkfm[64:128, :]), reads=["qkfm_k"], writes=["ktile"])
                p.emit("vector", lambda v: v.tensor_copy(ktile[64:128, :], qkfm[64:128, :]), reads=["qkfm_k"], writes=["ktile"])
                for c4 in range(4):
                    def mmbc(t, c4=c4):
                        return t.matmul(bc[0:64, 0:512], ctx["ones_row_bf"][0:1, 0:64],
                                        brow[0:1, c4 * 512:(c4 + 1) * 512], start=True, stop=True)
                    p.emit("tensor", mmbc, reads=["brow", "c_ones_row"], writes=["bc"])
                    p.emit("vector", lambda v, c4=c4: v.tensor_mul(
                        bkv[0:64, c4 * 512:(c4 + 1) * 512], qkfm[64:128, c4 * 512:(c4 + 1) * 512], bc[0:64, 0:512]),
                        reads=["bc", "qkfm_k"], writes=["bkv_k"])
                    p.emit("vector", lambda v, c4=c4: v.tensor_mul(
                        qkfm[0:64, c4 * 512:(c4 + 1) * 512], bkv[64:128, c4 * 512:(c4 + 1) * 512], bc[0:64, 0:512]),
                        reads=["bc", "bkv_v", "qdup"], writes=["qkfm_q"])
                for tc in range(2):
                    t0 = tc * TC
                    for m in range(NTL):
                        def mmba(t, m=m, t0=t0):
                            last = None
                            for n2 in range(2):
                                last = t.matmul(bc[:, n2 * 512:(n2 + 1) * 512], sel[:, m * 128:(m + 1) * 128],
                                                bkv[0:64, t0 + n2 * 512:t0 + (n2 + 1) * 512], start=True, stop=True)
                            return last
                        p.emit("tensor", mmba, reads=["bkv_k", "c_sel"], writes=["bc"])
                        p.emit("vector", lambda v, t0=t0: v.tensor_mul(Abuf[:], bc[:], ktile[:, t0:t0 + TC]),
                               reads=["bc", "ktile"], writes=["Abuf"])
                        p.emit("scalar", lambda s: s.activation(Abuf[:], Abuf[:], AF.Copy, scale=-1.0, bias=1.0),
                               reads=["Abuf"], writes=["Abuf"])
                        def mmbb(t, m=m, t0=t0):
                            last = None
                            for n2 in range(2):
                                last = t.matmul(bc[:, n2 * 512:(n2 + 1) * 512], sel[:, m * 128:(m + 1) * 128],
                                                qkfm[0:64, t0 + n2 * 512:t0 + (n2 + 1) * 512], start=True, stop=True)
                            return last
                        p.emit("tensor", mmbb, reads=["qkfm_q", "c_sel"], writes=["bc"])
                        p.emit("vector", lambda v, t0=t0: v.tensor_mul(Bbuf[:], bc[:], ktile[:, t0:t0 + TC]),
                               reads=["bc", "ktile"], writes=["Bbuf"])
                        if tc == 0:
                            p.emit("vector", lambda v: v.tensor_tensor_scan(Sbuf[:], Abuf[:], Bbuf[:], 0.0,
                                                                            ALU.mult, ALU.add),
                                   reads=["Abuf", "Bbuf"], writes=["Sbuf"])
                            p.emit("vector", lambda v, m=m: v.tensor_copy(carry[:, m:m + 1], Sbuf[:, TC - 1:TC]),
                                   reads=["Sbuf"], writes=[("carry", m)])
                        else:
                            p.emit("vector", lambda v, m=m: v.tensor_tensor_scan(Sbuf[:], Abuf[:], Bbuf[:],
                                                                                 carry[:, m:m + 1], ALU.mult, ALU.add),
                                   reads=["Abuf", "Bbuf", ("carry", m)], writes=["Sbuf"])
                        p.emit("gpsimd", lambda g, t0=t0: g.tensor_tensor(Mbuf[:], Sbuf[:], qdup[:, t0:t0 + TC], ALU.mult),
                               reads=["Sbuf", "qdup"], writes=["Mbuf"])
                        def mmr(t, m=m, tc=tc):
                            last = None
                            for n2 in range(2):
                                last = t.matmul(reds[tc * 2 + n2][:, 0:512], red[:, m * 64:(m + 1) * 64],
                                                Mbuf[:, n2 * 512:(n2 + 1) * 512], start=(m == 0), stop=(m == NTL - 1))
                            return last
                        p.emit("tensor", mmr, reads=["Mbuf", "c_red"],
                               writes=[("rd", tc * 2), ("rd", tc * 2 + 1)])
                for q_ in range(4):
                    p.emit("vector", lambda v, q_=q_: v.tensor_copy(stage[0:64, 0:512], reds[q_][:, 0:512]),
                           reads=[("rd", q_)], writes=["Abuf"])
                    p.emit("sync", lambda s, q_=q_, h=h: s.dma_start(
                        out=scr_ob[h * 64:(h + 1) * 64, q_ * 512:(q_ + 1) * 512], in_=stage[0:64, 0:512]),
                        reads=["Abuf"], writes=["scr_ob"], inc=16)
        p.barrier_reset()
        ar.off = mark
        ofm = ar.take([128, 1024], F32, f"scf_ofm{b}")
        otok = ar.take([128, 1024], F32, f"scf_otok{b}")
        gsil = ar.take([128, 1024], BF16, f"scf_gsil{b}")
        res = ar.take([128, 1024], BF16, f"scf_res{b}")
        resfm = res
        gbc = ar.take([128, 2048], F16, f"scf_gbc{b}")
        wslab2 = ar.take([128, KD * 1024], BF16, f"scf_w{b}")
        with ExitStack() as st:
            ps = std_psums(ctx, st)
            ps1, ps2, pt = ps["ps1"], ps["ps2"], ps["pt"]
            for gh in range(2):
                p.emit("sync", lambda s, gh=gh: s.dma_start(out=otok[:], in_=IN["L1_attn_b_lnb_bc"][:, gh * 1024:(gh + 1) * 1024]),
                       writes=["otok"], inc=16)
                p.emit("vector", lambda v, gh=gh: v.tensor_copy(gbc[:, gh * 1024:(gh + 1) * 1024], otok[:]),
                       reads=["otok"], writes=["gbc"])
            emit_load_w(ctx, IN["L1_attn_b_Wg_pack"], wslab2, "w2", D, D, w_sb_fo=D)
            for m in range(NT_B):
                ti = b * NT_B + m
                for k in range(KD):
                    p.emit("sync", lambda s, k=k, m=m: s.dma_start(
                        out=ofm[:, k * 128:(k + 1) * 128],
                        in_=scr_ob[k * 128:(k + 1) * 128, m * 128:(m + 1) * 128]),
                        writes=["ofm"], inc=16)
                def trof(t):
                    last = None
                    for k in range(KD):
                        last = t.transpose(ps1[:, k * 128:(k + 1) * 128], ofm[:, k * 128:(k + 1) * 128],
                                           ctx["ident_f32"])
                    return last
                p.emit("tensor", trof, reads=["ofm", "c_ident_f32"], writes=["ps1f"])
                p.emit("vector", lambda v: v.tensor_copy(otok[:], ps1[:]), reads=["ps1f"], writes=["otok"])
                for c2 in range(2):
                    p.emit("vector", lambda v, c2=c2: v.bn_stats(stat[:, 6 * c2:6 * c2 + 6], otok[:, 512 * c2:512 * (c2 + 1)]),
                           reads=["otok"], writes=[("st6", c2)])
                p.emit("vector", lambda v: v.bn_aggr(stat[:, 12:14], stat[:, 0:12]),
                       reads=[("st6", 0), ("st6", 1)], writes=["stmv"])
                p.emit("vector", lambda v: v.tensor_single_scalar(stat[:, 14:15], stat[:, 13:14], LN_EPS, ALU.add),
                       reads=["stmv"], writes=["strs"])
                p.emit("scalar", lambda s: s.activation(stat[:, 14:15], stat[:, 14:15], AF.Ln),
                       reads=["strs"], writes=["strs"])
                p.emit("scalar", lambda s: s.activation(stat[:, 14:15], stat[:, 14:15], AF.Exp, scale=-0.5),
                       reads=["strs"], writes=["strs"])
                p.emit("vector", lambda v: v.tensor_scalar(stat[:, 15:16], stat[:, 12:13], stat[:, 14:15], -1.0, ALU.mult, ALU.mult),
                       reads=["stmv", "strs"], writes=["stnm"])
                p.emit("vector", lambda v: v.tensor_scalar(otok[:], otok[:], stat[:, 14:15], stat[:, 15:16], ALU.mult, ALU.add),
                       reads=["otok", "strs", "stnm"], writes=["otok"])
                p.emit("vector", lambda v: v.tensor_mul(otok[:], otok[:], gbc[:, 0:1024]),
                       reads=["otok", "gbc"], writes=["otok"])
                p.emit("vector", lambda v: v.tensor_add(otok[:], otok[:], gbc[:, 1024:2048]),
                       reads=["otok", "gbc"], writes=["otok"])
                def mmgg(t, m=m):
                    last = None
                    for nh in (0, 512):
                        for k in range(KD):
                            last = t.matmul(ps2[:, nh:nh + 512],
                                            xfm[:, k * T_SEQ + m * 128:k * T_SEQ + (m + 1) * 128],
                                            wslab2[:, k * D + nh:k * D + nh + 512], start=(k == 0), stop=(k == KD - 1))
                    return last
                p.emit("tensor", mmgg, reads=[("xfm", k) for k in range(KD)] + [("w2", k) for k in range(KD)],
                       writes=["ps2f"])
                p.emit("scalar", lambda s: s.activation(gsil[:, 0:512], ps2[:, 0:512], AF.Silu),
                       reads=["ps2f"], writes=["gsil"])
                p.emit("scalar", lambda s: s.activation(gsil[:, 512:1024], ps2[:, 512:1024], AF.Silu),
                       reads=["ps2f"], writes=["gsil"])
                p.emit("vector", lambda v: v.tensor_mul(res[:], otok[:], gsil[:]),
                       reads=["otok", "gsil"], writes=["res"])
                p.emit("sync", lambda s, m=m: s.dma_start(out=scr_res[m * 128:(m + 1) * 128, 0:1024], in_=res[:]),
                       reads=["res"], writes=[("resd", m)], inc=16)
            emit_load_w(ctx, IN["L1_attn_b_Wout_pack"], wslab2, "w2", D, D, w_sb_fo=D)
            for m in range(NT_B):
                ti = b * NT_B + m
                p.emit("sync", lambda s, m=m: s.dma_start(out=res[:], in_=scr_res[m * 128:(m + 1) * 128, 0:1024]),
                       reads=[("resd", m)], writes=["res"], inc=16)
                def trres(t):
                    last = None
                    for k in range(KD):
                        last = t.transpose(pt[:, k * 128:(k + 1) * 128], res[:, k * 128:(k + 1) * 128],
                                           ctx["ident_bf"])
                    return last
                p.emit("tensor", trres, reads=["res", "c_ident"], writes=[("ptT", 0)])
                p.emit("vector", lambda v: v.tensor_copy(resfm[:].rearrange("p (k c) -> p k c", c=128),
                                                         pt[:, 0:1024].rearrange("p (k c) -> p k c", c=128)),
                       reads=[("ptT", 0)], writes=["res"])
                def mmout(t, m=m):
                    last = None
                    for nh in (0, 512):
                        for k in range(KD):
                            last = t.matmul(ps1[:, nh:nh + 512], resfm[:, k * 128:(k + 1) * 128],
                                            wslab2[:, k * D + nh:k * D + nh + 512], start=(k == 0), stop=(k == KD - 1))
                    return last
                p.emit("tensor", mmout, reads=["res"] + [("w2", k) for k in range(KD)], writes=["ps1f"])
                xs = x_sb[:, ti * D:(ti + 1) * D]
                p.emit("vector", lambda v, xs=xs: v.tensor_add(xs, xs, ps1[:]),
                       reads=["ps1f", ("x", ti)], writes=[("x", ti)])
        p.barrier_reset()






LSH_NH = 12
NCH = 64
MSK = -3.3e38
SELFV = -35.0
CLOC = 4.0
CLSH = 4.0
L0A_LSH_HEADS = list(range(4, 16))
L0A_LOC_HEADS = list(range(0, 4))
L0A_PHASES = "ABCDEF"
L0A_CSUB = 99
L0A_DSUB = 99


def _ln_affine(ctx, buf, key, lnbc, lnbc_key, which):
    p = ctx["p"]
    stat = ctx["stat"]
    for c2 in range(2):
        p.emit("vector", lambda v, c2=c2: v.bn_stats(stat[:, 6 * c2:6 * c2 + 6], buf[:, 512 * c2:512 * (c2 + 1)]),
               reads=[key], writes=[("st6", c2)])
    p.emit("vector", lambda v: v.bn_aggr(stat[:, 12:14], stat[:, 0:12]),
           reads=[("st6", 0), ("st6", 1)], writes=["stmv"])
    p.emit("vector", lambda v: v.tensor_single_scalar(stat[:, 14:15], stat[:, 13:14], LN_EPS, ALU.add),
           reads=["stmv"], writes=["strs"])
    p.emit("scalar", lambda s: s.activation(stat[:, 14:15], stat[:, 14:15], AF.Ln),
           reads=["strs"], writes=["strs"])
    p.emit("scalar", lambda s: s.activation(stat[:, 14:15], stat[:, 14:15], AF.Exp, scale=-0.5),
           reads=["strs"], writes=["strs"])
    p.emit("vector", lambda v: v.tensor_scalar(stat[:, 15:16], stat[:, 12:13], stat[:, 14:15], -1.0, ALU.mult, ALU.mult),
           reads=["stmv", "strs"], writes=["stnm"])
    p.emit("vector", lambda v: v.tensor_scalar(buf[:], buf[:], stat[:, 14:15], stat[:, 15:16], ALU.mult, ALU.add),
           reads=[key, "strs", "stnm"], writes=[key])
    p.emit("vector", lambda v: v.tensor_mul(buf[:], buf[:], lnbc[:, which * 2048:which * 2048 + 1024]),
           reads=[key, lnbc_key], writes=[key])
    p.emit("vector", lambda v: v.tensor_add(buf[:], buf[:], lnbc[:, which * 2048 + 1024:which * 2048 + 2048]),
           reads=[key, lnbc_key], writes=[key])


def build_l0_attn_a(ctx):
    from contextlib import ExitStack
    p = ctx["p"]
    nc = ctx["nc"]
    with ExitStack() as st:
        _PSN[0] += 1
        n_ = _PSN[0]
        ps = {}
        ps["pA"] = st.enter_context(nc.psum_tensor(f"lA_{n_}", [128, 512], F32)).ap()
        ps["pB"] = st.enter_context(nc.psum_tensor(f"lB_{n_}", [128, 512], F32)).ap()
        ps["pC"] = st.enter_context(nc.psum_tensor(f"lC_{n_}", [128, 512], F32)).ap()
        ps["pD"] = st.enter_context(nc.psum_tensor(f"lD_{n_}", [128, 512], F32)).ap()
        ps["ptb"] = st.enter_context(nc.psum_tensor(f"lpt_{n_}", [128, 1024], BF16)).ap()
        ps["pt32"] = st.enter_context(nc.psum_tensor(f"lp32_{n_}", [128, 1024], F32)).ap()
        for b in range(B_CORE):
            _l0a_batch(ctx, b, ps)
    p.barrier_reset()


def _l0a_batch(ctx, b, ps):
    p = ctx["p"]
    x_sb = ctx["x_sb"]
    IN = ctx["in"]
    nc = ctx["nc"]
    ar = ctx["ar"]
    stat = ctx["stat"]
    scr_xh = ctx["scr_y"]
    scr_qkv = ctx["scr_qkv"]
    scr_keys = ctx["scr_keys"]
    scr_att = ctx["scr_att"]
    scr_ao = ctx["scr_attnout"]
    scr_ff = ctx["scr_ob"].rearrange("a c -> (a c)").rearrange("(t d) -> t d", d=1024)
    pA, pB, pC, pD, ptb, pt32 = ps["pA"], ps["pB"], ps["pC"], ps["pD"], ps["ptb"], ps["pt32"]

    p.barrier_reset()
    ar.reset()
    xfm = ar.take([128, KD * T_SEQ], BF16, f"la_xfm{b}")
    hload = ar.take([128, KD * 128], BF16, f"la_hl{b}")
    qs32 = ar.take([128, 192], F32, f"la_qs{b}")
    stg = ar.take([128, 192], F32, f"la_stg{b}")
    rotsb = ar.take([64, 128], F32, f"la_rot{b}")
    iotad = ar.take([128, 128], F32, f"la_iod{b}")
    keyc = ar.take([128, 20], F32, f"la_keyc{b}")
    bkt = ar.take([128, 160], F32, f"la_bkt{b}")
    xh32 = ar.take([128, 1024], F32, f"la_xh32{b}")
    lqk = ar.take([64, 4096], BF16, f"la_lqk{b}")
    vaug = ar.take([128, NT_B * 65], BF16, f"la_vaug{b}")
    escr = ar.take([128, 256], F32, f"la_escr{b}")
    ebuf = ar.take([128, 256], BF16, f"la_ebuf{b}")
    scb = ar.take([128, 128], F32, f"la_scb{b}")
    locm = ar.take([128, 128], F32, f"la_locm{b}")
    aij = ar.take([128, 64], BF16, f"la_aij{b}")

    if len(L0A_LOC_HEADS) + len(L0A_LSH_HEADS) < 16:
        emit_zero(ctx, xh32[:], "xh32", 1024)
        for j0 in range(NT_B):
            p.emit("gpsimd", lambda g, j0=j0: g.dma_start(
                out=scr_ao[j0 * 128:(j0 + 1) * 128, 0:1024], in_=xh32[:]),
                reads=["xh32"], writes=["sao"], inc=16)
    p.emit("sync", lambda s: s.dma_start(out=rotsb[:], in_=IN["L0A_rot_pack"]), writes=["rot"], inc=16)
    p.emit("sync", lambda s: s.dma_start(out=locm[:], in_=IN["c_locmask"]), writes=["locm"], inc=16)
    p.emit("gpsimd", lambda g: g.iota(iotad[:], pattern=[[0, 4], [-1, 32]], base=31,
                                      channel_multiplier=0, allow_small_or_imprecise_dtypes=True),
           writes=["iotad"])
    p.emit("gpsimd", lambda g: g.iota(keyc[:, 0:4], pattern=[[1, 4]], base=0,
                                      channel_multiplier=0, allow_small_or_imprecise_dtypes=True),
           writes=["keyc"])
    p.emit("vector", lambda v: v.tensor_single_scalar(keyc[:, 0:4], keyc[:, 0:4], 262145.0, ALU.mult),
           reads=["keyc"], writes=["keyc"])
    p.emit("gpsimd", lambda g: g.iota(keyc[:, 4:20], pattern=[[512, 16]], base=0,
                                      channel_multiplier=4, allow_small_or_imprecise_dtypes=True),
           writes=["keyc2"])
    p.emit("gpsimd", lambda g: g.iota(
        vaug.rearrange("p (k c) -> p k c", c=65)[:, :, 64:65], pattern=[[0, NT_B], [0, 1]],
        base=1, channel_multiplier=0, allow_small_or_imprecise_dtypes=True),
        writes=["vones"])

    for j in range(NT_B):
        ti = b * NT_B + j
        emit_ln_tile(ctx, x_sb[:, ti * D:(ti + 1) * D], ("x", ti), xh32[:], "xh32")
        p.emit("sync", lambda s, j=j: s.dma_start(out=scr_xh[j * 128:(j + 1) * 128, 0:1024], in_=xh32[:]),
               reads=["xh32"], writes=[("sxh", j)], inc=16)

        def tr32(t):
            last = None
            for k in range(KD):
                last = t.transpose(pt32[:, k * 128:(k + 1) * 128], xh32[:, k * 128:(k + 1) * 128],
                                   ctx["ident_f32"])
            return last
        p.emit("tensor", tr32, reads=["xh32", "c_ident_f32"], writes=["pt32"])
        dst = xfm.rearrange("p (k t) -> p k t", t=T_SEQ)[:, 0:KD, j * 128:(j + 1) * 128]
        p.emit("vector", lambda v, dst=dst: v.tensor_copy(dst, pt32[:].rearrange("p (k t) -> p k t", t=128)),
               reads=["pt32"], writes=[("xfm", k2) for k2 in range(KD)])

    for h in L0A_LOC_HEADS + L0A_LSH_HEADS:
        lsh = h >= 4
        for (woff, dst0) in ((h * 64, 0), (1024 + h * 64, 64)):
            for k in range(KD):
                p.emit("gpsimd", lambda g, k=k, woff=woff, dst0=dst0: g.dma_start(
                    out=hload[:, k * 128 + dst0:k * 128 + dst0 + 64],
                    in_=IN["L0A_wqkv_pack"][k * 128:(k + 1) * 128, woff:woff + 64]),
                    writes=[("hl", k)], inc=16)
        for si in range(NT_B):
            def mmp(t, si=si):
                last = None
                for k in range(KD):
                    last = t.matmul(pA[:, 0:128], xfm[:, k * T_SEQ + si * 128:k * T_SEQ + (si + 1) * 128],
                                    hload[:, k * 128:(k + 1) * 128], start=(k == 0), stop=(k == KD - 1))
                return last
            p.emit("tensor", mmp, reads=[("hl", k) for k in range(KD)] + [("xfm", k) for k in range(KD)],
                   writes=["pAp"])
            p.emit("vector", lambda v: v.tensor_copy(qs32[:, 0:64], pA[:, 0:64]), reads=["pAp"], writes=["qs32"])
            p.emit("scalar", lambda s: s.activation(qs32[:, 128:192], pA[:, 0:64], AF.Square),
                   reads=["pAp"], writes=["qs32"])
            p.emit("vector", lambda v: v.tensor_reduce(stat[:, 0:1], qs32[:, 128:192], AX.X, ALU.add),
                   reads=["qs32"], writes=["nss"])
            p.emit("scalar", lambda s: s.activation(stat[:, 1:2], stat[:, 0:1], AF.Sqrt),
                   reads=["nss"], writes=["nsr"])
            p.emit("vector", lambda v: v.reciprocal(stat[:, 1:2], stat[:, 1:2]),
                   reads=["nsr"], writes=["nsr"])
            if lsh:
                p.emit("vector", lambda v: v.tensor_single_scalar(stat[:, 1:2], stat[:, 1:2], 0.125, ALU.mult),
                       reads=["nsr"], writes=["nsr"])
            p.emit("vector", lambda v: v.tensor_single_scalar(qs32[:, 64:128], pA[:, 0:64], stat[:, 1:2], ALU.mult),
                   reads=["pAp", "nsr"], writes=["qs32"])
            if lsh:
                p.emit("vector", lambda v: v.tensor_copy(stg[:, 0:128], qs32[:, 0:128]),
                       reads=["qs32"], writes=["stg"])
                p.emit("vector", lambda v: v.tensor_copy(stg[:, 128:192], pA[:, 64:128]),
                       reads=["pAp"], writes=["stg"])
                p.emit("sync", lambda s, si=si, h=h: s.dma_start(
                    out=scr_qkv[(h - 4) * 2048 + si * 128:(h - 4) * 2048 + (si + 1) * 128, :], in_=stg[:]),
                    reads=["stg"], writes=["sqkv"], inc=16)

                def trq(t):
                    return t.transpose(pt32[0:64, 0:128], qs32[:, 0:64], ctx["ident_f32"])
                p.emit("tensor", trq, reads=["qs32", "c_ident_f32"], writes=["pt32"])
                p.emit("vector", lambda v: v.tensor_copy(xh32[0:64, 0:128], pt32[0:64, 0:128]),
                       reads=["pt32"], writes=["xh32"])

                def mmrot(t):
                    return t.matmul(pB[:, 0:128], xh32[0:64, 0:128], rotsb[:], start=True, stop=True)
                p.emit("tensor", mmrot, reads=["xh32", "rot"], writes=["pBa"])
                rv = pB[:, 0:128].rearrange("p (r c) -> p r c", c=32)
                p.emit("vector", lambda v, rv=rv: v.tensor_reduce(bkt[:, 0:4], rv, AX.X, ALU.max),
                       reads=["pBa"], writes=["bk_mp"])
                p.emit("vector", lambda v, rv=rv: v.tensor_reduce(bkt[:, 4:8], rv, AX.X, ALU.min),
                       reads=["pBa"], writes=["bk_mn"])
                p.emit("vector", lambda v: v.tensor_copy(
                    bkt[:, 8:136].rearrange("p (r c) -> p r c", c=32),
                    bkt[:, 0:4].rearrange("p (r o) -> p r o", o=1).broadcast_to((128, 4, 32))),
                    reads=["bk_mp"], writes=["bk_mpb"])
                p.emit("vector", lambda v: v.tensor_tensor(escr[:, 0:128], pB[:, 0:128], bkt[:, 8:136], ALU.is_ge),
                       reads=["pBa", "bk_mpb"], writes=["escr"])
                p.emit("vector", lambda v: v.tensor_mul(escr[:, 0:128], escr[:, 0:128], iotad[:]),
                       reads=["escr", "iotad"], writes=["escr"])
                p.emit("vector", lambda v: v.tensor_reduce(bkt[:, 136:140],
                                                           escr[:, 0:128].rearrange("p (r c) -> p r c", c=32),
                                                           AX.X, ALU.max),
                       reads=["escr"], writes=["bk_ipm"])
                p.emit("vector", lambda v: v.tensor_scalar(bkt[:, 136:140], bkt[:, 136:140], -1.0, 31.0, ALU.mult, ALU.add),
                       reads=["bk_ipm"], writes=["bk_ipm"])
                p.emit("vector", lambda v: v.tensor_copy(
                    bkt[:, 8:136].rearrange("p (r c) -> p r c", c=32),
                    bkt[:, 4:8].rearrange("p (r o) -> p r o", o=1).broadcast_to((128, 4, 32))),
                    reads=["bk_mn"], writes=["bk_mpb"])
                p.emit("vector", lambda v: v.tensor_tensor(escr[:, 0:128], pB[:, 0:128], bkt[:, 8:136], ALU.is_le),
                       reads=["pBa", "bk_mpb"], writes=["escr"])
                p.emit("vector", lambda v: v.tensor_mul(escr[:, 0:128], escr[:, 0:128], iotad[:]),
                       reads=["escr", "iotad"], writes=["escr"])
                p.emit("vector", lambda v: v.tensor_reduce(bkt[:, 140:144],
                                                           escr[:, 0:128].rearrange("p (r c) -> p r c", c=32),
                                                           AX.X, ALU.max),
                       reads=["escr"], writes=["bk_inm"])
                p.emit("vector", lambda v: v.tensor_scalar(bkt[:, 140:144], bkt[:, 140:144], -1.0, 63.0, ALU.mult, ALU.add),
                       reads=["bk_inm"], writes=["bk_inm"])
                p.emit("vector", lambda v: v.tensor_single_scalar(bkt[:, 144:148], bkt[:, 4:8], -1.0, ALU.mult),
                       reads=["bk_mn"], writes=["bk_nmn"])
                p.emit("vector", lambda v: v.tensor_tensor(bkt[:, 148:152], bkt[:, 0:4], bkt[:, 144:148], ALU.is_ge),
                       reads=["bk_mp", "bk_nmn"], writes=["bk_ge"])
                p.emit("vector", lambda v: v.tensor_mul(bkt[:, 152:156], bkt[:, 148:152], bkt[:, 136:140]),
                       reads=["bk_ge", "bk_ipm"], writes=["bk_b1"])
                p.emit("vector", lambda v: v.tensor_scalar(bkt[:, 148:152], bkt[:, 148:152], -1.0, 1.0, ALU.mult, ALU.add),
                       reads=["bk_ge"], writes=["bk_ge"])
                p.emit("vector", lambda v: v.tensor_mul(bkt[:, 156:160], bkt[:, 148:152], bkt[:, 140:144]),
                       reads=["bk_ge", "bk_inm"], writes=["bk_b2"])
                p.emit("vector", lambda v: v.tensor_add(bkt[:, 152:156], bkt[:, 152:156], bkt[:, 156:160]),
                       reads=["bk_b1", "bk_b2"], writes=["bk_b1"])
                p.emit("vector", lambda v: v.tensor_single_scalar(bkt[:, 152:156], bkt[:, 152:156], 8192.0, ALU.mult),
                       reads=["bk_b1"], writes=["bk_b1"])
                p.emit("vector", lambda v: v.tensor_add(bkt[:, 152:156], bkt[:, 152:156], keyc[:, 0:4]),
                       reads=["bk_b1", "keyc"], writes=["bk_b1"])
                p.emit("vector", lambda v, si=si: v.tensor_single_scalar(
                    bkt[:, 152:156], bkt[:, 152:156], keyc[:, 4 + si:5 + si], ALU.add),
                    reads=["bk_b1", "keyc2"], writes=["bk_b1"])
                p.emit("sync", lambda s, si=si, h=h: s.dma_start(
                    out=scr_keys[h - 4:h - 3, :].rearrange("o (t r) -> o t r", r=4)[:, si * 128:(si + 1) * 128, :],
                    in_=bkt[:, 152:156]),
                    reads=["bk_b1"], writes=["skeys"], inc=16)
            else:
                p.emit("vector", lambda v: v.tensor_copy(aij[:, 0:64], qs32[:, 0:64]),
                       reads=["qs32"], writes=["aij"])
                p.emit("vector", lambda v: v.tensor_copy(ebuf[:, 0:64], qs32[:, 64:128]),
                       reads=["qs32"], writes=["ebuf"])

                def trl2(t):
                    t.transpose(ptb[0:64, 0:128], aij[:, 0:64], ctx["ident_bf"])
                    return t.transpose(ptb[0:64, 128:256], ebuf[:, 0:64], ctx["ident_bf"])
                p.emit("tensor", trl2, reads=["aij", "ebuf", "c_ident"], writes=[("ptT", 0)])
                p.emit("vector", lambda v, si=si: v.tensor_copy(lqk[:, si * 128:(si + 1) * 128], ptb[0:64, 0:128]),
                       reads=[("ptT", 0)], writes=["lqk"])
                p.emit("vector", lambda v, si=si: v.tensor_copy(lqk[:, 2048 + si * 128:2048 + (si + 1) * 128],
                                                                ptb[0:64, 128:256]),
                       reads=[("ptT", 0)], writes=["lqk"])
                p.emit("vector", lambda v, si=si: v.tensor_copy(
                    vaug.rearrange("p (k c) -> p k c", c=65)[:, si:si + 1, 0:64],
                    pA[:, 64:128].rearrange("p (o c) -> p o c", c=64)),
                    reads=["pAp"], writes=["vaugv"])
        if not lsh:
            for w in range(NT_B):
                q0 = w * 128

                def mmdc(t, w=w, q0=q0):
                    last = t.matmul(pA[:, 0:128], lqk[:, 2048 + q0:2048 + q0 + 128], lqk[:, q0:q0 + 128],
                                    start=True, stop=True)
                    if w > 0:
                        last = t.matmul(pA[:, 128:256], lqk[:, 2048 + q0 - 128:2048 + q0], lqk[:, q0:q0 + 128],
                                        start=True, stop=True)
                    return last
                p.emit("tensor", mmdc, reads=["lqk"], writes=["pAp"])
                p.emit("vector", lambda v: v.scalar_tensor_tensor(escr[:, 0:128], pA[:, 0:128], 0.125,
                                                                  locm[:, 0:128], ALU.mult, ALU.add),
                       reads=["pAp", "locm"], writes=["escr"])
                p.emit("scalar", lambda s: s.activation(ebuf[:, 0:128], escr[:, 0:128], AF.Exp, bias=-CLOC),
                       reads=["escr"], writes=["ebuf"])
                if w > 0:
                    p.emit("vector", lambda v: v.tensor_single_scalar(escr[:, 128:256], pA[:, 128:256], 0.125, ALU.mult),
                           reads=["pAp"], writes=["escr"])
                    p.emit("scalar", lambda s: s.activation(ebuf[:, 128:256], escr[:, 128:256], AF.Exp, bias=-CLOC),
                           reads=["escr"], writes=["ebuf"])

                def mmpv(t, w=w):
                    last = t.matmul(pB[0:65, 0:128], vaug[:, w * 65:(w + 1) * 65], ebuf[:, 0:128],
                                    start=True, stop=(w == 0))
                    if w > 0:
                        last = t.matmul(pB[0:65, 0:128], vaug[:, (w - 1) * 65:w * 65], ebuf[:, 128:256],
                                        start=False, stop=True)
                    return last
                p.emit("tensor", mmpv, reads=["ebuf", "vaugv", "vones"], writes=["pBa"])
                p.emit("vector", lambda v: v.tensor_single_scalar(scb[0:1, 0:128], pB[64:65, 0:128], 1e-30, ALU.add),
                       reads=["pBa"], writes=["scb"])
                p.emit("vector", lambda v: v.reciprocal(scb[0:1, 0:128], scb[0:1, 0:128]),
                       reads=["scb"], writes=["scb"])
                p.emit("vector", lambda v: v.tensor_copy(escr[0:1, 0:128], scb[0:1, 0:128]),
                       reads=["scb"], writes=["escr"])

                def mmbc(t):
                    return t.matmul(pC[0:64, 0:128], ctx["ones_row_f32"][0:1, 0:64], escr[0:1, 0:128],
                                    start=True, stop=True)
                p.emit("tensor", mmbc, reads=["escr", "c_ones_row_f32"], writes=["pCb"])
                p.emit("vector", lambda v: v.tensor_copy(scb[0:64, 0:128], pC[0:64, 0:128]),
                       reads=["pCb"], writes=["scb"])
                p.emit("vector", lambda v: v.scalar_tensor_tensor(ebuf[0:64, 0:128], pB[0:64, 0:128], 1.0,
                                                                  scb[0:64, 0:128], ALU.bypass, ALU.mult),
                       reads=["pBa", "scb"], writes=["ebuf"])

                def trat(t):
                    return t.transpose(ptb[:, 0:64], ebuf[0:64, 0:128], ctx["ident_bf"][0:64, 0:64])
                p.emit("tensor", trat, reads=["ebuf", "c_ident"], writes=[("ptT", 0)])
                p.emit("vector", lambda v: v.tensor_copy(aij[:, 0:64], ptb[:, 0:64]),
                       reads=[("ptT", 0)], writes=["aij"])
                if w == 0:
                    p.emit("vector", lambda v: v.tensor_copy(
                        aij[0:1, 0:64], vaug.rearrange("p (k c) -> p k c", c=65)[0:1, 0:1, 0:64]),
                        reads=["vaugv"], writes=["aij"])
                p.emit("sync", lambda s, w=w, h=h: s.dma_start(
                    out=scr_ao[w * 128:(w + 1) * 128, h * 64:(h + 1) * 64], in_=aij[:, 0:64]),
                    reads=["aij"], writes=["sao"], inc=16)

    if "B" not in L0A_PHASES:
        return
    p.barrier_reset()
    ar.reset()
    stT = ar.take([128, 768], F32, f"lb_stT{b}")
    rtT = ar.take([128, 768], F32, f"lb_rtT{b}")
    stTi = ar.take([128, 768], I32, f"lb_stTi{b}")
    rtTi = ar.take([128, 768], I32, f"lb_rtTi{b}")
    NLH = len(L0A_LSH_HEADS)
    hofs_i = ar.take([NLH, 2], F32, f"lb_hof{b}")
    stf = ar.take([NLH, 256], F32, f"lb_stf{b}")
    markB = ar.off
    off_ping = ar.off
    ping = ar.take([NLH, 8192], F32, f"lb_ping{b}")
    off_pong = ar.off
    pong = ar.take([NLH, 8192], F32, f"lb_pong{b}")
    ping_i = ar.alias(off_ping, [NLH, 8192], I32, f"lb_pingi{b}")
    pong_i = ar.alias(off_pong, [NLH, 8192], I32, f"lb_pongi{b}")
    p.emit("gpsimd", lambda g: g.iota(hofs_i[:, 0:1], pattern=[[0, 1]], base=0,
                                      channel_multiplier=2048, allow_small_or_imprecise_dtypes=True),
           writes=["hofs"])
    p.emit("gpsimd", lambda g: g.iota(hofs_i[:, 1:2], pattern=[[0, 1]], base=0,
                                      channel_multiplier=8192, allow_small_or_imprecise_dtypes=True),
           writes=["hofs"])
    p.emit("sync", lambda s: s.dma_start(out=ping[:], in_=scr_keys[0:NLH, :]), writes=["ping"], inc=16)
    cur, oth = ping, pong
    curk, othk = "ping", "pong"
    m = 1
    while m < 8192:
        p.emit("vector", lambda v, m=m, cur=cur, oth=oth: v.tensor_tensor(
            oth.rearrange("l (b2 two mm) -> l b2 two mm", two=2, mm=m)[:, :, 0, :],
            cur.rearrange("l (b2 two mm) -> l b2 two mm", two=2, mm=m)[:, :, 0, :],
            cur.rearrange("l (b2 mm) -> l b2 mm", mm=2 * m)[:, :, ::-1][:, :, 0:m],
            ALU.min), reads=[curk], writes=[othk])
        p.emit("vector", lambda v, m=m, cur=cur, oth=oth: v.tensor_tensor(
            oth.rearrange("l (b2 two mm) -> l b2 two mm", two=2, mm=m)[:, :, 1, :][:, :, ::-1],
            cur.rearrange("l (b2 two mm) -> l b2 two mm", two=2, mm=m)[:, :, 0, :],
            cur.rearrange("l (b2 mm) -> l b2 mm", mm=2 * m)[:, :, ::-1][:, :, 0:m],
            ALU.max), reads=[curk], writes=[othk])
        cur, oth = oth, cur
        curk, othk = othk, curk
        j2 = m // 2
        while j2 >= 1:
            p.emit("vector", lambda v, j2=j2, cur=cur, oth=oth: v.tensor_tensor(
                oth.rearrange("l (b2 two jj) -> l b2 two jj", two=2, jj=j2)[:, :, 0, :],
                cur.rearrange("l (b2 two jj) -> l b2 two jj", two=2, jj=j2)[:, :, 0, :],
                cur.rearrange("l (b2 two jj) -> l b2 two jj", two=2, jj=j2)[:, :, 1, :],
                ALU.min), reads=[curk], writes=[othk])
            p.emit("vector", lambda v, j2=j2, cur=cur, oth=oth: v.tensor_tensor(
                oth.rearrange("l (b2 two jj) -> l b2 two jj", two=2, jj=j2)[:, :, 1, :],
                cur.rearrange("l (b2 two jj) -> l b2 two jj", two=2, jj=j2)[:, :, 0, :],
                cur.rearrange("l (b2 two jj) -> l b2 two jj", two=2, jj=j2)[:, :, 1, :],
                ALU.max), reads=[curk], writes=[othk])
            cur, oth = oth, cur
            curk, othk = othk, curk
            j2 //= 2
        m *= 2
    ci = ping_i if cur is ping else pong_i
    oi = ping_i if oth is ping else pong_i
    p.emit("vector", lambda v, oi=oi, cur=cur: v.tensor_copy(oi[:], cur[:]),
           reads=[curk], writes=[othk])
    p.emit("vector", lambda v, ci=ci, oi=oi: v.tensor_single_scalar(ci[:], oi[:], 8191, ALU.bitwise_and),
           reads=[othk], writes=[curk])
    p.emit("vector", lambda v, ci=ci, oi=oi: v.tensor_single_scalar(oi[:], ci[:], 3, ALU.bitwise_and),
           reads=[curk], writes=[othk])
    p.emit("vector", lambda v, ci=ci: v.tensor_single_scalar(ci[:], ci[:], 2, ALU.arith_shift_right),
           reads=[curk], writes=[curk])
    p.emit("vector", lambda v, oi=oi: v.tensor_single_scalar(oi[:], oi[:], 11, ALU.logical_shift_left),
           reads=[othk], writes=[othk])
    p.emit("vector", lambda v, ci=ci, oi=oi: v.tensor_tensor(oi[:], oi[:], ci[:], ALU.add),
           reads=[curk, othk], writes=[othk])
    for c in range(NCH):
        p.emit("vector", lambda v, c=c, ci=ci: v.tensor_copy(stf[:, 0:128], ci[:, c * 128:(c + 1) * 128]),
               reads=[curk], writes=["stf"])
        p.emit("vector", lambda v, c=c, oi=oi: v.tensor_copy(stf[:, 128:256], oi[:, c * 128:(c + 1) * 128]),
               reads=[othk], writes=["stf"])
        p.emit("vector", lambda v: v.tensor_single_scalar(stf[:, 0:128], stf[:, 0:128],
                                                          hofs_i[:, 0:1], ALU.add),
               reads=["stf", "hofs"], writes=["stf"])
        p.emit("vector", lambda v: v.tensor_single_scalar(stf[:, 128:256], stf[:, 128:256],
                                                          hofs_i[:, 1:2], ALU.add),
               reads=["stf", "hofs"], writes=["stf"])

        def trst(t, c=c):
            t.transpose(pt32[:, 0:NLH], stf[:, 0:128], ctx["ident_f32"][0:NLH, 0:NLH])
            return t.transpose(pt32[:, 128:128 + NLH], stf[:, 128:256],
                               ctx["ident_f32"][0:NLH, 0:NLH])
        p.emit("tensor", trst, reads=["stf", "c_ident_f32"], writes=["pt32"])
        p.emit("vector", lambda v, c=c: v.tensor_copy(stT[:, c * 12:c * 12 + NLH], pt32[:, 0:NLH]),
               reads=["pt32"], writes=["stT"])
        p.emit("vector", lambda v, c=c: v.tensor_copy(rtT[:, c * 12:c * 12 + NLH], pt32[:, 128:128 + NLH]),
               reads=["pt32"], writes=["rtT"])
    stT3 = stT.rearrange("p (c h) -> p c h", h=12)
    rtT3 = rtT.rearrange("p (c h) -> p c h", h=12)
    p.emit("vector", lambda v: v.tensor_copy(
        stTi.rearrange("p (c h) -> p c h", h=12)[:, :, 0:NLH], stT3[:, :, 0:NLH]),
        reads=["stT"], writes=["stTi"])
    p.emit("vector", lambda v: v.tensor_copy(
        rtTi.rearrange("p (c h) -> p c h", h=12)[:, :, 0:NLH], rtT3[:, :, 0:NLH]),
        reads=["rtT"], writes=["rtTi"])

    if "C" not in L0A_PHASES:
        return
    p.barrier_reset()
    ar.off = markB
    gb = ar.take([128, 3 * 192], F32, f"lc_gb{b}")
    qkr = ar.take([64, 3 * 256], BF16, f"lc_qkr{b}")
    vro = ar.take([128, 3 * 65], BF16, f"lc_vro{b}")
    stkc = ar.take([128, 1], F32, f"lc_stk{b}")
    idxc = ar.take([64, 1], I32, f"lc_idxc{b}")
    mlt = ar.take([128, 64], F32, f"lc_mlt{b}")
    meq = ar.take([128, 64], F32, f"lc_meq{b}")
    esc = ar.take([128, 64], F32, f"lc_esc{b}")
    eb2 = ar.take([128, 64], BF16, f"lc_eb2{b}")
    srow = ar.take([1, 128], F32, f"lc_srow{b}")
    ot65 = ar.take([128, 65], F32, f"lc_ot{b}")
    p.emit("gpsimd", lambda g: g.iota(
        vro.rearrange("p (k c) -> p k c", c=65)[:, :, 64:65], pattern=[[0, 3], [0, 1]],
        base=1, channel_multiplier=0, allow_small_or_imprecise_dtypes=True),
        writes=["vones2"])

    for hix, h in enumerate(L0A_LSH_HEADS):
        hh = h - 4

        def gather(g_, slot):
            idxcol = stTi[:, g_ * 12 + hh:g_ * 12 + hh + 1]
            p.emit("gpsimd", lambda g2, g_=g_, slot=slot, idxcol=idxcol: g2.indirect_dma_start(
                out=gb[:, slot * 192:(slot + 1) * 192], out_offset=None,
                in_=scr_qkv,
                in_offset=IndirectOffsetOnAxis(ap=idxcol, axis=0)),
                reads=["stTi", "sqkv"], writes=[("gb", slot)], inc=16)

            def trg2(t, slot=slot):
                t.transpose(pt32[0:64, 0:128], gb[:, slot * 192:slot * 192 + 64], ctx["ident_f32"])
                return t.transpose(pt32[0:64, 128:256], gb[:, slot * 192 + 64:slot * 192 + 128], ctx["ident_f32"])
            p.emit("tensor", trg2, reads=[("gb", slot), "c_ident_f32"], writes=["pt32"])
            p.emit("vector", lambda v, slot=slot: v.tensor_copy(qkr[:, slot * 256:slot * 256 + 256], pt32[0:64, 0:256]),
                   reads=["pt32"], writes=[("qkr", slot)])
            p.emit("vector", lambda v, slot=slot: v.tensor_copy(
                vro.rearrange("p (k c) -> p k c", c=65)[:, slot:slot + 1, 0:64],
                gb[:, slot * 192 + 128:slot * 192 + 192].rearrange("p (o c) -> p o c", c=64)),
                reads=[("gb", slot)], writes=[("vro", slot)])

        def attn_pair(g_, slot, slotp):
            if L0A_CSUB < 2:
                return
            def trsq(t, g_=g_):
                return t.transpose(pt32[0:1, 256:384], stT[:, g_ * 12 + hh:g_ * 12 + hh + 1],
                                   ctx["ident_f32"])
            p.emit("tensor", trsq, reads=["stT", "c_ident_f32"], writes=["pt32"])
            p.emit("vector", lambda v: v.tensor_copy(srow[0:1, 0:128], pt32[0:1, 256:384]),
                   reads=["pt32"], writes=["srow"])

            def mmsq(t):
                return t.matmul(pC[:, 0:128], ctx["ones_row_f32"][0:1, 0:128], srow[0:1, 0:128],
                                start=True, stop=True)
            p.emit("tensor", mmsq, reads=["srow", "c_ones_row_f32"], writes=["pCq"])
            if L0A_CSUB < 3:
                return
            for half in range(2):
                q0 = half * 64
                if half == 0:
                    def mmd(t, slot=slot, slotp=slotp, q0=q0):
                        t.matmul(pA[0:64, 0:64], qkr[:, slot * 256 + 128:slot * 256 + 192],
                                 qkr[:, slot * 256 + q0:slot * 256 + q0 + 64], start=True, stop=True)
                        return t.matmul(pA[64:128, 0:64], qkr[:, slotp * 256 + 192:slotp * 256 + 256],
                                        qkr[:, slot * 256 + q0:slot * 256 + q0 + 64], start=True, stop=True)
                    p.emit("tensor", mmd, reads=[("qkr", slot), ("qkr", slotp)], writes=["pAd"])
                    gp = (g_ - 1) % NCH
                    p.emit("vector", lambda v, g_=g_: v.tensor_copy(stkc[0:64, 0:1],
                                                                    stT[0:64, g_ * 12 + hh:g_ * 12 + hh + 1]),
                           reads=["stT"], writes=["stkc"])
                    p.emit("vector", lambda v, gp=gp: v.tensor_copy(stkc[64:128, 0:1],
                                                                    stT[64:128, gp * 12 + hh:gp * 12 + hh + 1]),
                           reads=["stT"], writes=["stkc"])
                    stkap = stkc[:, 0:1]
                    stkk = "stkc"
                else:
                    def mmd(t, slot=slot, q0=q0):
                        return t.matmul(pA[:, 0:64], qkr[:, slot * 256 + 128:slot * 256 + 256],
                                        qkr[:, slot * 256 + q0:slot * 256 + q0 + 64], start=True, stop=True)
                    p.emit("tensor", mmd, reads=[("qkr", slot)], writes=["pAd"])
                    stkap = stT[:, g_ * 12 + hh:g_ * 12 + hh + 1]
                    stkk = "stT"
                p.emit("vector", lambda v, q0=q0, stkap=stkap: v.tensor_single_scalar(
                    mlt[:, 0:64], pC[:, q0:q0 + 64], stkap, ALU.is_lt),
                    reads=["pCq", stkk], writes=["mlt"])
                p.emit("vector", lambda v, q0=q0, stkap=stkap: v.tensor_single_scalar(
                    meq[:, 0:64], pC[:, q0:q0 + 64], stkap, ALU.is_equal),
                    reads=["pCq", stkk], writes=["meq"])
                p.emit("vector", lambda v: v.scalar_tensor_tensor(esc[:, 0:64], mlt[:, 0:64], MSK, pA[:, 0:64],
                                                                  ALU.mult, ALU.add),
                       reads=["mlt", "pAd"], writes=["esc"])
                p.emit("vector", lambda v: v.scalar_tensor_tensor(esc[:, 0:64], meq[:, 0:64], SELFV, esc[:, 0:64],
                                                                  ALU.mult, ALU.add),
                       reads=["meq", "esc"], writes=["esc"])
                p.emit("scalar", lambda s: s.activation(eb2[:, 0:64], esc[:, 0:64], AF.Exp, bias=-CLSH),
                       reads=["esc"], writes=["eb2"])
                if L0A_CSUB < 4:
                    continue
                if half == 0:
                    def mmpv2(t, slot=slot, slotp=slotp):
                        t.matmul(pB[0:65, 0:64], vro[0:64, slot * 65:(slot + 1) * 65], eb2[0:64, 0:64],
                                 start=True, stop=True)
                        return t.matmul(pD[0:65, 0:64], vro[64:128, slotp * 65:(slotp + 1) * 65],
                                        eb2[64:128, 0:64], start=True, stop=True)
                    p.emit("tensor", mmpv2, reads=["eb2", ("vro", slot), ("vro", slotp), "vones2"], writes=["pBo2"])
                    p.emit("vector", lambda v: v.tensor_copy(ot65[0:65, 0:64], pD[0:65, 0:64]),
                           reads=["pBo2"], writes=["ot65"])
                    p.emit("vector", lambda v: v.scalar_tensor_tensor(ot65[0:65, 0:64], pB[0:65, 0:64], 1.0,
                                                                      ot65[0:65, 0:64], ALU.bypass, ALU.add),
                           reads=["pBo2", "ot65"], writes=["ot65"])
                else:
                    def mmpv2(t, slot=slot):
                        return t.matmul(pB[0:65, 0:64], vro[:, slot * 65:(slot + 1) * 65], eb2[:, 0:64],
                                        start=True, stop=True)
                    p.emit("tensor", mmpv2, reads=["eb2", ("vro", slot), "vones2"], writes=["pBo2"])
                    p.emit("vector", lambda v: v.tensor_copy(ot65[0:65, 0:64], pB[0:65, 0:64]),
                           reads=["pBo2"], writes=["ot65"])

                if L0A_CSUB < 41:
                    continue
                def trot(t):
                    return t.transpose(pt32[0:64, 384:449], ot65[0:65, 0:64], ctx["ident_f32"][0:65, 0:65])
                p.emit("tensor", trot, reads=["ot65", "c_ident_f32"], writes=["pt32"])
                p.emit("vector", lambda v: v.tensor_copy(ot65[0:64, 0:65], pt32[0:64, 384:449]),
                       reads=["pt32"], writes=["ot65"])
                if L0A_CSUB < 42:
                    continue
                p.emit("vector", lambda v: v.tensor_single_scalar(stat[0:64, 2:3], ot65[0:64, 64:65], 1e-30, ALU.add),
                       reads=["ot65"], writes=["dnc"])
                p.emit("vector", lambda v: v.reciprocal(stat[0:64, 2:3], stat[0:64, 2:3]),
                       reads=["dnc"], writes=["dnc"])
                p.emit("vector", lambda v: v.tensor_single_scalar(ot65[0:64, 0:64], ot65[0:64, 0:64],
                                                                  stat[0:64, 2:3], ALU.mult),
                       reads=["ot65", "dnc"], writes=["ot65"])
                if L0A_CSUB < 5:
                    continue
                src_sl = rtTi[64 * half:64 * half + 64, g_ * 12 + hh:g_ * 12 + hh + 1]
                p.emit("vector", lambda v, src_sl=src_sl: v.tensor_copy(idxc[:, 0:1], src_sl),
                       reads=["rtTi"], writes=["idxc"])
                p.emit("gpsimd", lambda g2: g2.indirect_dma_start(
                    out=scr_att,
                    out_offset=IndirectOffsetOnAxis(ap=idxc[:, 0:1], axis=0),
                    in_=ot65[0:64, 0:65], in_offset=None),
                    reads=["ot65", "idxc"], writes=["satt"], inc=16)

        gather(0, 0)
        gather(1, 1)
        attn_pair(1, 1, 0)
        for g_ in range(2, NCH):
            slot = g_ % 3
            gather(g_, slot)
            attn_pair(g_, slot, (g_ - 1) % 3)
        gather(0, 1)
        attn_pair(0, 1, (NCH - 1) % 3)

    if "D" not in L0A_PHASES:
        return
    p.barrier_reset()
    cmb = ar.take([128, 4 * 65], F32, f"ld_cmb{b}")
    att2 = ar.take([128, 64], BF16, f"ld_att{b}")
    for h in L0A_LSH_HEADS:
        hh = h - 4
        for j in range(NT_B):
            for r in range(4):
                p.emit("sync", lambda s, hh=hh, j=j, r=r: s.dma_start(
                    out=cmb[:, r * 65:(r + 1) * 65],
                    in_=scr_att[hh * 8192 + r * 2048 + j * 128: hh * 8192 + r * 2048 + (j + 1) * 128, :]),
                    writes=["cmb"], inc=16)
            if L0A_DSUB < 2:
                continue
            lsev = cmb.rearrange("p (r c) -> p r c", c=65)[:, :, 64:65]
            p.emit("vector", lambda v, lsev=lsev: v.tensor_copy(
                stat[:, 4:8].rearrange("p (r o) -> p r o", o=1), lsev),
                reads=["cmb"], writes=["lse"])
            p.emit("vector", lambda v: v.tensor_single_scalar(stat[:, 4:8], stat[:, 4:8], 1e-30, ALU.add),
                   reads=["lse"], writes=["lse"])
            p.emit("scalar", lambda s: s.activation(stat[:, 4:8], stat[:, 4:8], AF.Ln),
                   reads=["lse"], writes=["lse"])
            if L0A_DSUB < 3:
                continue
            p.emit("vector", lambda v: v.tensor_reduce(stat[:, 8:9], stat[:, 4:8], AX.X, ALU.max),
                   reads=["lse"], writes=["lsm"])
            p.emit("vector", lambda v: v.tensor_single_scalar(stat[:, 9:10], stat[:, 8:9], -1.0, ALU.mult),
                   reads=["lsm"], writes=["lsm"])
            p.emit("scalar", lambda s: s.activation(stat[:, 4:8], stat[:, 4:8], AF.Exp, bias=stat[:, 9:10],
                                                    accum_out=stat[:, 10:11]),
                   reads=["lse", "lsm"], writes=["lse"])
            p.emit("vector", lambda v: v.reciprocal(stat[:, 10:11], stat[:, 10:11]),
                   reads=["lse"], writes=["lsum"])
            p.emit("vector", lambda v: v.tensor_single_scalar(stat[:, 4:8], stat[:, 4:8], stat[:, 10:11], ALU.mult),
                   reads=["lse", "lsum"], writes=["lse"])
            if L0A_DSUB < 4:
                continue
            p.emit("vector", lambda v: v.tensor_single_scalar(cmb[:, 0:64], cmb[:, 0:64], stat[:, 4:5], ALU.mult),
                   reads=["cmb", "lse"], writes=["cmb"])
            for r in range(1, 4):
                p.emit("vector", lambda v, r=r: v.tensor_single_scalar(cmb[:, r * 65:r * 65 + 64],
                                                                       cmb[:, r * 65:r * 65 + 64],
                                                                       stat[:, 4 + r:5 + r], ALU.mult),
                       reads=["cmb", "lse"], writes=["cmb"])
                p.emit("vector", lambda v, r=r: v.tensor_add(cmb[:, 0:64], cmb[:, 0:64], cmb[:, r * 65:r * 65 + 64]),
                       reads=["cmb"], writes=["cmb"])
            p.emit("vector", lambda v: v.tensor_copy(att2[:], cmb[:, 0:64]), reads=["cmb"], writes=["att2"])
            p.emit("sync", lambda s, j=j, h=h: s.dma_start(
                out=scr_ao[j * 128:(j + 1) * 128, h * 64:(h + 1) * 64], in_=att2[:]),
                reads=["att2"], writes=["sao"], inc=16)

    if "E" not in L0A_PHASES:
        return
    p.barrier_reset()
    ar.reset()
    xfm = ar.take([128, KD * T_SEQ], BF16, f"le_xfm{b}")
    aor = ar.take([128, 1024], BF16, f"le_aor{b}")
    aofm = ar.take([128, 1024], BF16, f"le_aofm{b}")
    x1t = ar.take([128, 1024], F32, f"le_x1{b}")
    lnbc = ar.take([128, 4096], F32, f"le_lnbc{b}")
    borow = ar.take([1, 1024], BF16, f"le_bo{b}")
    markE = ar.off
    wsl = ar.take([128, KD * 1024], BF16, f"le_w{b}")
    emit_load_w(ctx, IN["L0A_wout_pack"], wsl, "lwout", D, D, w_sb_fo=D)
    p.emit("gpsimd", lambda g: g.dma_start(out=borow[0:1, :], in_=IN["L0A_bout"][None, :]), writes=["borow"], inc=16)
    p.emit("sync", lambda s: s.dma_start(out=lnbc[:], in_=IN["L0A_ln_bc"]), writes=["lnbc"], inc=16)
    for j in range(NT_B):
        p.emit("sync", lambda s, j=j: s.dma_start(out=aor[:], in_=scr_ao[j * 128:(j + 1) * 128, 0:1024]),
               reads=["sao"], writes=["aor"], inc=16)

        def trao(t):
            last = None
            for k in range(KD):
                last = t.transpose(ptb[:, k * 128:(k + 1) * 128], aor[:, k * 128:(k + 1) * 128], ctx["ident_bf"])
            return last
        p.emit("tensor", trao, reads=["aor", "c_ident"], writes=[("ptT", 0)])
        p.emit("vector", lambda v: v.tensor_copy(aofm[:].rearrange("p (k c) -> p k c", c=128),
                                                 ptb[:, 0:1024].rearrange("p (k c) -> p k c", c=128)),
               reads=[("ptT", 0)], writes=["aofm"])

        def mmsa(t, j=j):
            last = None
            for nh in (0, 512):
                dst = pA[:, 0:512] if nh == 0 else pB[:, 0:512]
                for k in range(KD):
                    t.matmul(dst, aofm[:, k * 128:(k + 1) * 128], wsl[:, k * D + nh:k * D + nh + 512],
                             start=(k == 0), stop=False)
                last = t.matmul(dst, ctx["ones_row_bf"][0:1, 0:128], borow[0:1, nh:nh + 512],
                                start=False, stop=True)
            return last
        p.emit("tensor", mmsa, reads=["aofm", "borow", "c_ones_row"] + [("lwout", k) for k in range(KD)],
               writes=["pAe"])
        p.emit("sync", lambda s, j=j: s.dma_start(out=x1t[:], in_=scr_xh[j * 128:(j + 1) * 128, 0:1024]),
               reads=[("sxh", j)], writes=["x1t"], inc=16)
        p.emit("vector", lambda v: v.tensor_add(x1t[:, 0:512], x1t[:, 0:512], pA[:, 0:512]),
               reads=["pAe", "x1t"], writes=["x1t"])
        p.emit("vector", lambda v: v.tensor_add(x1t[:, 512:1024], x1t[:, 512:1024], pB[:, 0:512]),
               reads=["pAe", "x1t"], writes=["x1t"])
        _ln_affine(ctx, x1t, "x1t", lnbc, "lnbc", 0)
        p.emit("sync", lambda s, j=j: s.dma_start(out=scr_xh[2048 + j * 128:2048 + (j + 1) * 128, 0:1024], in_=x1t[:]),
               reads=["x1t"], writes=[("sx1", j)], inc=16)

        def trx1(t):
            last = None
            for k in range(KD):
                last = t.transpose(pt32[:, k * 128:(k + 1) * 128], x1t[:, k * 128:(k + 1) * 128], ctx["ident_f32"])
            return last
        p.emit("tensor", trx1, reads=["x1t", "c_ident_f32"], writes=["pt32"])
        dst = xfm.rearrange("p (k t) -> p k t", t=T_SEQ)[:, 0:KD, j * 128:(j + 1) * 128]
        p.emit("vector", lambda v, dst=dst: v.tensor_copy(dst, pt32[:].rearrange("p (k t) -> p k t", t=128)),
               reads=["pt32"], writes=[("xfm", k2) for k2 in range(KD)])

    if "F" not in L0A_PHASES:
        return
    p.barrier_reset()
    ar.off = markE
    wgu = ar.take([128, 1024], BF16, f"lf_wgu{b}")
    hbl = ar.take([128, 2048], BF16, f"lf_h{b}")
    w2b = ar.take([128, 1024], BF16, f"lf_w2{b}")
    bcol = ar.take([128, 32], F32, f"lf_bc{b}")
    eo = ar.take([128, 1024], F32, f"lf_eo{b}")
    b2row = ar.take([1, 1024], BF16, f"lf_b2{b}")
    p.emit("sync", lambda s: s.dma_start(out=bcol[:], in_=IN["L0A_bff1_cols"]), writes=["bcol"], inc=16)
    p.emit("gpsimd", lambda g: g.dma_start(out=b2row[0:1, :], in_=IN["L0A_bff2"][None, :]), writes=["b2row"], inc=16)
    for ih in range(32):
        p.emit("gpsimd", lambda g, ih=ih: g.dma_start(out=wgu[:], in_=IN["L0A_wff1_pack"][:, ih * 1024:(ih + 1) * 1024]),
               writes=["wgu"], inc=16)
        p.emit("gpsimd", lambda g, ih=ih: g.dma_start(out=w2b[:], in_=IN["L0A_wff2_pack"][:, ih * 1024:(ih + 1) * 1024]),
               writes=["w2b"], inc=16)
        for c0 in range(0, T_SEQ, 512):
            def mmh(t, ih=ih, c0=c0):
                last = None
                for k in range(KD):
                    last = t.matmul(pC[:, 0:512], wgu[:, k * 128:(k + 1) * 128],
                                    xfm[:, k * T_SEQ + c0:k * T_SEQ + c0 + 512], start=(k == 0), stop=(k == KD - 1))
                return last
            p.emit("tensor", mmh, reads=["wgu"] + [("xfm", k) for k in range(KD)], writes=["pCf"])
            p.emit("scalar", lambda s, ih=ih, c0=c0: s.activation(hbl[:, c0:c0 + 512], pC[:, 0:512], AF.Gelu,
                                                                  bias=bcol[:, ih:ih + 1]),
                   reads=["pCf", "bcol"], writes=["hbl"])
        for mj in range(NT_B):
            for fh in range(2):
                def mmd2(t, mj=mj, fh=fh, ih=ih):
                    last = t.matmul(pD[:, 0:512], hbl[:, mj * 128:(mj + 1) * 128],
                                    w2b[:, fh * 512:(fh + 1) * 512], start=True, stop=(ih != 0))
                    if ih == 0:
                        last = t.matmul(pD[:, 0:512], ctx["ones_row_bf"][0:1, 0:128],
                                        b2row[0:1, fh * 512:(fh + 1) * 512], start=False, stop=True)
                    return last
                p.emit("tensor", mmd2, reads=["hbl", "w2b", "b2row", "c_ones_row"], writes=["pDf"])
                p.emit("vector", lambda v, fh=fh: v.tensor_copy(eo[:, fh * 512:(fh + 1) * 512], pD[:, 0:512]),
                       reads=["pDf"], writes=["eo"])
            yap = scr_ff[mj * 128:(mj + 1) * 128, :]
            if ih == 0:
                p.emit("gpsimd", lambda g, yap=yap: g.dma_start(out=yap, in_=eo[:]),
                       reads=["eo"], writes=[("ff", mj)], inc=16)
            else:
                p.emit("gpsimd", lambda g, yap=yap: g.dma_start(out=yap, in_=eo[:], accum_op=ALU.add),
                       reads=["eo", ("ff", mj)], writes=[("ff", mj)], inc=16)
    for j in range(NT_B):
        ti = b * NT_B + j
        p.emit("sync", lambda s, j=j: s.dma_start(out=x1t[:], in_=scr_xh[2048 + j * 128:2048 + (j + 1) * 128, 0:1024]),
               reads=[("sx1", j)], writes=["x1t"], inc=16)
        p.emit("sync", lambda s, j=j: s.dma_start(out=eo[:], in_=scr_ff[j * 128:(j + 1) * 128, :]),
               reads=[("ff", j)], writes=["eo"], inc=16)
        p.emit("vector", lambda v: v.tensor_add(x1t[:], x1t[:], eo[:]),
               reads=["x1t", "eo"], writes=["x1t"])
        _ln_affine(ctx, x1t, "x1t", lnbc, "lnbc", 1)
        xs = x_sb[:, ti * D:(ti + 1) * D]
        p.emit("vector", lambda v, xs=xs: v.tensor_add(xs, xs, x1t[:]),
               reads=["x1t", ("x", ti)], writes=[("x", ti)])



SUBLAYERS = {
    "l0_attn_b": build_l0_attn_b,
    "l1_attn_a": build_l1_attn_a,
    "l0_ffn": build_l0_ffn,
    "l1_ffn": build_l1_ffn,
    "l1_attn_b": build_l1_attn_b,
    "l0_attn_a": build_l0_attn_a,
}

INPUT_SHAPES = {
    "L0_attn_b_Wq": (1024, 1024), "L0_attn_b_bq": (1024,),
    "L0_attn_b_Wk": (1024, 1024), "L0_attn_b_bk": (1024,),
    "L0_attn_b_Wv": (1024, 1024), "L0_attn_b_bv": (1024,),
    "L1_attn_a_W_qkv_hpack": (1024, 3072), "L1_attn_a_b_qkv": (3072,),
    "L1_attn_a_W_out_pack": (1024, 1024), "L1_attn_a_b_out": (1024,),
    "L0_ffn_gate_pack": (128, 8 * 8),
    "L0_ffn_wg_pack": (8, 128, 11 * 8 * 128), "L0_ffn_wu_pack": (8, 128, 11 * 8 * 128),
    "L0_ffn_wd_pack": (8, 128, 11 * 1024),
    "L0_ffn_swg_pack": (128, 22 * 8 * 128), "L0_ffn_swu_pack": (128, 22 * 8 * 128),
    "L0_ffn_swd_pack": (2, 128, 22 * 512),
    "L1_ffn_gate_pack": (128, 8 * 17),
    "L1_ffn_wg_pack": (16, 128, 4 * 8 * 128), "L1_ffn_wu_pack": (16, 128, 4 * 8 * 128),
    "L1_ffn_wd_pack": (16, 128, 4 * 1024),
    "L1_ffn_swg_pack": (128, 22 * 8 * 128), "L1_ffn_swu_pack": (128, 22 * 8 * 128),
    "L1_ffn_swd_pack": (2, 128, 22 * 512),
    "L1_attn_b_W_hpack": (1024, 16 * 192), "L1_attn_b_Wbeta_pack": (1024, 16),
    "L1_attn_b_Wg_pack": (1024, 1024), "L1_attn_b_Wout_pack": (1024, 1024),
    "L1_attn_b_lnb_bc": (128, 2048), "c_selred": (128, 32 * 128 + 32 * 64),
    "L0A_wqkv_pack": (1024, 2048), "L0A_rot_pack": (64, 128),
    "L0A_wout_pack": (1024, 1024), "L0A_bout": (1024,),
    "L0A_wff1_pack": (128, 32 * 1024), "L0A_bff1_cols": (128, 32),
    "L0A_wff2_pack": (128, 32 * 1024), "L0A_bff2": (1024,),
    "L0A_ln_bc": (128, 4096), "c_locmask": (128, 128),
}


def build_program(sublayers, input_names):
    nc = bass.Bass("TRN2", target_bir_lowering=False)
    p = Prog(nc)
    ctx = {"nc": nc, "p": p, "sb": {}, "ps": {}, "in": {}}
    for nm in input_names:
        dt_ = BF16 if nm == "c_selred" else F32
        ctx["in"][nm] = nc.dram_tensor(nm, list(INPUT_SHAPES[nm]), dt_, kind="ExternalInput").ap()
    x_in = nc.dram_tensor("x_in", [T_CORE, D], F32, kind="ExternalInput").ap()
    x_out = nc.dram_tensor("x_out", [T_CORE, D], F32, kind="ExternalOutput").ap()
    ctx["scr_attnout"] = nc.dram_tensor("scr_attnout", [B_CORE * 1024, 2048], BF16, kind="Internal").ap()
    ctx["scr_y"] = nc.dram_tensor("scr_y", [T_CORE, D], F32, kind="Internal").ap()
    ctx["scr_ob"] = nc.dram_tensor("scr_ob", [1024, 2048], F32, kind="Internal").ap()
    nlh = len(L0A_LSH_HEADS)
    ctx["scr_qkv"] = nc.dram_tensor("scr_qkv", [nlh * 2048, 192], F32, kind="Internal").ap()
    ctx["scr_keys"] = nc.dram_tensor("scr_keys", [nlh, 8192], F32, kind="Internal").ap()
    ctx["scr_att"] = nc.dram_tensor("scr_att", [nlh * 8192, 65], F32, kind="Internal").ap()

    A = nc.alloc_sbuf_tensor
    ctx["x_sb"] = A("sb_x", [128, NT * D], F32).ap()
    ctx["stat"] = A("sb_stat", [128, 16], F32).ap()
    ctx["cols"] = A("sb_cols", [128, 64], F32).ap()
    ctx["ar"] = Arena(nc)

    declare_consts(ctx)
    constcol = nc.alloc_sbuf_tensor("sb_constcol", [128, 1], F32).ap()
    emit_const_col(ctx, constcol, "constcol1", 1)
    nc.const_aps.register(F32, 1.0, constcol)
    constcol4 = nc.alloc_sbuf_tensor("sb_constcol4", [128, 1], F32).ap()
    emit_const_col(ctx, constcol4, "constcol4", -4)
    nc.const_aps.register(F32, -4.0, constcol4)
    assert nc.sbuf_bytes_remaining >= ARENA_BYTES, nc.sbuf_bytes_remaining

    p.emit("sync", lambda s: s.dma_start(
        out=ctx["x_sb"][:].rearrange("p (t d) -> p t d", d=D), in_=x_in.rearrange("(t p) d -> p t d", p=128)),
        writes=[("x", i) for i in range(NT)], inc=16)
    p.barrier_reset()

    for sl in sublayers:
        SUBLAYERS[sl](ctx)
        p.barrier_reset()

    p.emit("sync", lambda s: s.dma_start(
        out=x_out.rearrange("(t p) d -> p t d", p=128), in_=ctx["x_sb"][:].rearrange("p (t d) -> p t d", d=D)),
        reads=[("x", i) for i in range(NT)], inc=16)
    p.finish("sync")
    return nc



SEQ = ["l0_attn_a", "l0_ffn", "l0_attn_b", "l0_ffn",
       "l1_attn_a", "l1_ffn", "l1_attn_b", "l1_ffn"]
GROUPS = [SEQ]

SL_INPUTS = {
    "l0_attn_a": ["L0A_wqkv_pack", "L0A_rot_pack", "L0A_wout_pack", "L0A_bout",
                  "L0A_wff1_pack", "L0A_bff1_cols", "L0A_wff2_pack", "L0A_bff2",
                  "L0A_ln_bc", "c_locmask"],
    "l0_ffn": ["L0_ffn_gate_pack", "L0_ffn_wg_pack", "L0_ffn_wu_pack", "L0_ffn_wd_pack",
               "L0_ffn_swg_pack", "L0_ffn_swu_pack", "L0_ffn_swd_pack"],
    "l0_attn_b": ["L0_attn_b_Wq", "L0_attn_b_bq", "L0_attn_b_Wk", "L0_attn_b_bk",
                  "L0_attn_b_Wv", "L0_attn_b_bv"],
    "l1_attn_a": ["L1_attn_a_W_qkv_hpack", "L1_attn_a_b_qkv",
                  "L1_attn_a_W_out_pack", "L1_attn_a_b_out"],
    "l1_ffn": ["L1_ffn_gate_pack", "L1_ffn_wg_pack", "L1_ffn_wu_pack", "L1_ffn_wd_pack",
               "L1_ffn_swg_pack", "L1_ffn_swu_pack", "L1_ffn_swd_pack"],
    "l1_attn_b": ["L1_attn_b_W_hpack", "L1_attn_b_Wbeta_pack", "L1_attn_b_Wg_pack",
                  "L1_attn_b_Wout_pack", "L1_attn_b_lnb_bc", "c_selred"],
}


def pack_kblocks(w, nih):
    return np.ascontiguousarray(
        w.reshape(8, 128, nih, 128).transpose(2, 1, 0, 3).reshape(nih, 128, 1024)
        .transpose(1, 0, 2).reshape(128, nih * 1024))


def pack_rows(w, nih, fo=1024):
    return np.ascontiguousarray(w.reshape(nih, 128, fo).transpose(1, 0, 2).reshape(128, nih * fo))


def pack_gate(g, EG):
    return np.ascontiguousarray(g.reshape(8, 128, EG).transpose(1, 0, 2).reshape(128, 8 * EG))


def make_locmask():
    kk = np.arange(128)[:, None]
    qq = np.arange(128)[None, :]
    m = np.where(kk > qq, np.float32(-3.3e38), np.float32(0.0))
    m = np.where(kk == qq, np.float32(-50000.0), m)
    return np.ascontiguousarray(m.astype(np.float32))


def make_selred():
    import ml_dtypes
    NTL = 32
    sel = np.zeros((128, NTL * 128), dtype=ml_dtypes.bfloat16)
    red = np.zeros((128, NTL * 64), dtype=ml_dtypes.bfloat16)
    for m in range(NTL):
        for p2 in range(128):
            i = 2 * m + p2 // 64
            sel[i, m * 128 + p2] = 1
            red[p2, m * 64 + i] = 1
    out = np.zeros((128, NTL * 128 + NTL * 64), dtype=ml_dtypes.bfloat16)
    out[0:64, 0:NTL * 128] = sel[0:64]
    out[:, NTL * 128:] = red
    return out


def pack_all(ins):
    f32 = np.float32
    pk = {}
    pk["L0A_wqkv_pack"] = np.ascontiguousarray(
        np.concatenate([ins["L0_attn_a_w_qk"].T, ins["L0_attn_a_w_v"].T], axis=1)).astype(f32)
    pk["L0A_rot_pack"] = np.ascontiguousarray(
        ins["L0_attn_a_rotations"].reshape(64, 128)).astype(f32)
    pk["L0A_wout_pack"] = np.ascontiguousarray(ins["L0_attn_a_w_out"].T).astype(f32)
    pk["L0A_bout"] = np.ascontiguousarray(ins["L0_attn_a_b_out"]).astype(f32)
    w1t = np.ascontiguousarray(ins["L0_attn_a_w_ff1"].T)
    pk["L0A_wff1_pack"] = np.ascontiguousarray(
        w1t.reshape(8, 128, 32, 128).transpose(2, 1, 0, 3).reshape(32, 128, 1024)
        .transpose(1, 0, 2).reshape(128, 32 * 1024)).astype(f32)
    pk["L0A_bff1_cols"] = np.ascontiguousarray(
        ins["L0_attn_a_b_ff1"].reshape(32, 128).T).astype(f32)
    w2t = np.ascontiguousarray(ins["L0_attn_a_w_ff2"].T)
    pk["L0A_wff2_pack"] = np.ascontiguousarray(
        w2t.reshape(32, 128, 1024).transpose(1, 0, 2).reshape(128, 32 * 1024)).astype(f32)
    pk["L0A_bff2"] = np.ascontiguousarray(ins["L0_attn_a_b_ff2"]).astype(f32)
    gb = np.concatenate([ins["L0_attn_a_ln1_g"], ins["L0_attn_a_ln1_b"],
                         ins["L0_attn_a_ln2_g"], ins["L0_attn_a_ln2_b"]])[None, :]
    pk["L0A_ln_bc"] = np.ascontiguousarray(np.broadcast_to(gb, (128, 4096))).astype(f32)
    pk["c_locmask"] = make_locmask()
    pk["L0_ffn_gate_pack"] = pack_gate(np.ascontiguousarray(ins["L0_ffn_gate_w"].T), 8).astype(f32)
    pk["L0_ffn_wg_pack"] = np.stack(
        [pack_kblocks(ins["L0_ffn_wg"][e], 11) for e in range(8)]).astype(f32)
    pk["L0_ffn_wu_pack"] = np.stack(
        [pack_kblocks(ins["L0_ffn_wu"][e], 11) for e in range(8)]).astype(f32)
    pk["L0_ffn_wd_pack"] = np.stack(
        [pack_rows(ins["L0_ffn_wd"][e], 11) for e in range(8)]).astype(f32)
    pk["L0_ffn_swg_pack"] = pack_kblocks(ins["L0_ffn_swg"], 22).astype(f32)
    pk["L0_ffn_swu_pack"] = pack_kblocks(ins["L0_ffn_swu"], 22).astype(f32)
    pk["L0_ffn_swd_pack"] = np.stack(
        [pack_rows(ins["L0_ffn_swd"][:, fh * 512:(fh + 1) * 512], 22, fo=512)
         for fh in range(2)]).astype(f32)
    for nm in ("Wq", "bq", "Wk", "bk", "Wv", "bv"):
        pk[f"L0_attn_b_{nm}"] = np.ascontiguousarray(ins[f"L0_attn_b_{nm}"]).astype(f32)
    Wp = np.ascontiguousarray(ins["L1_attn_a_W_qkv"].T)
    hp = np.concatenate([np.concatenate(
        [Wp[:, h * 64:(h + 1) * 64], Wp[:, 1024 + h * 64:1024 + (h + 1) * 64],
         Wp[:, 2048 + h * 64:2048 + (h + 1) * 64]], axis=1) for h in range(16)], axis=1)
    pk["L1_attn_a_W_qkv_hpack"] = np.ascontiguousarray(hp).astype(f32)
    pk["L1_attn_a_b_qkv"] = np.ascontiguousarray(ins["L1_attn_a_b_qkv"]).astype(f32)
    pk["L1_attn_a_W_out_pack"] = np.ascontiguousarray(ins["L1_attn_a_W_out"].T).astype(f32)
    pk["L1_attn_a_b_out"] = np.ascontiguousarray(ins["L1_attn_a_b_out"]).astype(f32)
    gp = np.concatenate([np.ascontiguousarray(ins["L1_ffn_gate_w"].T),
                         np.ascontiguousarray(ins["L1_ffn_shared_gate_w"].T)], axis=1)
    pk["L1_ffn_gate_pack"] = pack_gate(gp, 17).astype(f32)
    wgu = ins["L1_ffn_w_gate_up"]
    pk["L1_ffn_wg_pack"] = np.stack(
        [pack_kblocks(np.ascontiguousarray(wgu[e][:, :512]), 4) for e in range(16)]).astype(f32)
    pk["L1_ffn_wu_pack"] = np.stack(
        [pack_kblocks(np.ascontiguousarray(wgu[e][:, 512:]), 4) for e in range(16)]).astype(f32)
    pk["L1_ffn_wd_pack"] = np.stack(
        [pack_rows(ins["L1_ffn_w_down"][e], 4) for e in range(16)]).astype(f32)
    pk["L1_ffn_swg_pack"] = pack_kblocks(
        np.ascontiguousarray(ins["L1_ffn_shared_w_gate"].T), 22).astype(f32)
    pk["L1_ffn_swu_pack"] = pack_kblocks(
        np.ascontiguousarray(ins["L1_ffn_shared_w_up"].T), 22).astype(f32)
    swdT = np.ascontiguousarray(ins["L1_ffn_shared_w_down"].T)
    pk["L1_ffn_swd_pack"] = np.stack(
        [pack_rows(swdT[:, fh * 512:(fh + 1) * 512], 22, fo=512) for fh in range(2)]).astype(f32)
    hp2 = np.concatenate([np.concatenate(
        [ins["L1_attn_b_Wq"][:, h * 64:(h + 1) * 64],
         ins["L1_attn_b_Wk"][:, h * 64:(h + 1) * 64],
         ins["L1_attn_b_Wv"][:, h * 64:(h + 1) * 64]], axis=1) for h in range(16)], axis=1)
    pk["L1_attn_b_W_hpack"] = np.ascontiguousarray(hp2).astype(f32)
    pk["L1_attn_b_Wbeta_pack"] = np.ascontiguousarray(ins["L1_attn_b_Wbeta"]).astype(f32)
    pk["L1_attn_b_Wg_pack"] = np.ascontiguousarray(ins["L1_attn_b_Wg"]).astype(f32)
    pk["L1_attn_b_Wout_pack"] = np.ascontiguousarray(ins["L1_attn_b_Wout"]).astype(f32)
    gb2 = np.concatenate([ins["L1_attn_b_ln_g"], ins["L1_attn_b_ln_b"]])[None, :]
    pk["L1_attn_b_lnb_bc"] = np.ascontiguousarray(np.broadcast_to(gb2, (128, 2048))).astype(f32)
    pk["c_selred"] = make_selred()
    return pk


_PROG_CACHE = {}


def kernel(**inputs):
    from concourse.bass_utils import run_bass_kernel_spmd
    pk = pack_all(inputs)
    consts = make_consts()
    x = np.asarray(inputs["L0_attn_a_x"], dtype=np.float32)
    xs = [np.ascontiguousarray(x[2 * c:2 * c + 2].reshape(T_CORE, D)) for c in range(N_CORES)]
    for grp in GROUPS:
        names = sorted({nm for sl in grp for nm in SL_INPUTS[sl]})
        key = (tuple(grp), tuple(names))
        if key not in _PROG_CACHE:
            _PROG_CACHE[key] = build_program(list(grp), names)
        nc = _PROG_CACHE[key]
        in_maps = []
        for c in range(N_CORES):
            m = dict(consts)
            for nm in names:
                m[nm] = pk[nm]
            m["x_in"] = xs[c]
            in_maps.append(m)
        res = run_bass_kernel_spmd(nc, in_maps, core_ids=list(range(N_CORES)))
        xs = [np.asarray(res.results[c]["x_out"], dtype=np.float32) for c in range(N_CORES)]
    return np.concatenate([s.reshape(2, T_SEQ, D) for s in xs], axis=0)
